# version 01 (final)
# baseline (speedup 1.0000x reference)
import math
import os
import numpy as np
import ml_dtypes
import concourse.bass as bass
import concourse.mybir as mybir
from concourse.bass import IndirectOffsetOnAxis
from concourse.ap import AP as BassAP
from concourse.bass_utils import run_bass_kernel_spmd

F32 = mybir.dt.float32
BF16 = mybir.dt.bfloat16
U32 = mybir.dt.uint32
I32 = mybir.dt.int32
AF = mybir.ActivationFunctionType
ALU = mybir.AluOpType
AX = mybir.AxisListType

S_LEN = 4096
D = 1024
NTCH = S_LEN // 128
NDCH = D // 128
BF = ml_dtypes.bfloat16


ENGINES = ("pe", "act", "dve", "pool", "sp")
DMA_QUEUES = ("sp", "pool", "act")
NSLOT = 8


class Sched:
    def __init__(self, nc):
        self.nc = nc
        self.prog = {e: [] for e in ENGINES}
        self.count = {e: 0 for e in ENGINES}
        self.sems = {}
        self.dma_sems = {}
        self.dma_uses = {q: [0] * NSLOT for q in DMA_QUEUES}
        self.dma_n = {q: 0 for q in DMA_QUEUES}
        self.writer = {}
        self.readers = {}
        self.waited = {e: {} for e in ENGINES}
        self.dwaited = {e: {} for e in ENGINES}
        self.alias = {}
        self._ctxs = []

    def _res(self, keys):
        return tuple(self.alias.get(k, k) for k in keys)

    def alloc_sems(self, ctx):
        for e in ENGINES:
            self.sems[e] = ctx.enter_context(self.nc.semaphore(f"sem_{e}"))
        for q in DMA_QUEUES:
            for s in range(NSLOT):
                self.dma_sems[(q, s)] = ctx.enter_context(
                    self.nc.semaphore(f"dsem_{q}{s}"))

    def _need(self, eng, dep):
        kindt = dep[0]
        if kindt == 'e':
            _, src, seq = dep
            if src == eng and src == 'pe':
                return
            if self.waited[eng].get(src, -1) >= seq and src != eng:
                return
            if src == eng:
                if self.waited[eng].get(src, -1) >= seq:
                    return
            self.waited[eng][src] = max(self.waited[eng].get(src, -1), seq)
            sem = self.sems[src]
            val = seq + 1
            self.prog[eng].append(lambda E, sem=sem, val=val: E.wait_ge(sem, val))
        else:
            _, q, slot, use = dep
            val = use * 16
            if self.dwaited[eng].get((q, slot), 0) >= val:
                return
            self.dwaited[eng][(q, slot)] = val
            sem = self.dma_sems[(q, slot)]
            self.prog[eng].append(lambda E, sem=sem, val=val: E.wait_ge(sem, val))

    def op(self, eng, fn, reads=(), writes=()):
        reads = self._res(reads)
        writes = self._res(writes)
        for k in reads:
            w = self.writer.get(k)
            if w is not None:
                self._need(eng, w)
        for k in writes:
            w = self.writer.get(k)
            if w is not None:
                self._need(eng, w)
            for r in self.readers.get(k, ()):
                self._need(eng, r)
        seq = self.count[eng]
        sem = self.sems[eng]

        def emit(E, fn=fn, sem=sem):
            inst = fn(E)
            assert inst is not None, "op fn must return its last instruction"
            inst.then_inc(sem, 1)
        self.prog[eng].append(emit)
        self.count[eng] += 1
        tok = ('e', eng, seq)
        for k in writes:
            self.writer[k] = tok
            self.readers[k] = []
        for k in reads:
            self.readers.setdefault(k, []).append(tok)
        return tok

    @staticmethod
    def _autokeys(out, in_, reads, writes):
        reads = list(reads)
        writes = list(writes)
        try:
            if str(in_.space).endswith("DRAM"):
                reads.append("D_" + in_.name)
        except Exception:
            pass
        try:
            if str(out.space).endswith("DRAM"):
                writes.append("D_" + out.name)
        except Exception:
            pass
        return tuple(reads), tuple(writes)

    def dma(self, q, out, in_, reads=(), writes=(), **kw):
        reads, writes = self._autokeys(out, in_, reads, writes)
        reads = self._res(reads)
        writes = self._res(writes)
        for k in reads:
            w = self.writer.get(k)
            if w is not None:
                self._need(q, w)
        for k in writes:
            w = self.writer.get(k)
            if w is not None:
                self._need(q, w)
            for r in self.readers.get(k, ()):
                self._need(q, r)
        slot = self.dma_n[q] % NSLOT
        prev_use = self.dma_uses[q][slot]
        if prev_use > 0:
            self._need(q, ('d', q, slot, prev_use))
        sem = self.dma_sems[(q, slot)]

        def emit(E, out=out, in_=in_, sem=sem, kw=kw):
            E.dma_start(out=out, in_=in_, **kw).then_inc(sem, 16)
        self.prog[q].append(emit)
        self.dma_uses[q][slot] = prev_use + 1
        self.dma_n[q] += 1
        tok = ('d', q, slot, prev_use + 1)
        for k in writes:
            self.writer[k] = tok
            self.readers[k] = []
        for k in reads:
            self.readers.setdefault(k, []).append(tok)
        return tok

    def idma(self, out, out_offset, in_, in_offset, reads=(), writes=(), **kw):
        q = 'pool'
        reads, writes = self._autokeys(out, in_, reads, writes)
        reads = self._res(reads)
        writes = self._res(writes)
        for k in reads:
            w = self.writer.get(k)
            if w is not None:
                self._need(q, w)
        for k in writes:
            w = self.writer.get(k)
            if w is not None:
                self._need(q, w)
            for r in self.readers.get(k, ()):
                self._need(q, r)
        slot = self.dma_n[q] % NSLOT
        prev_use = self.dma_uses[q][slot]
        if prev_use > 0:
            self._need(q, ('d', q, slot, prev_use))
        sem = self.dma_sems[(q, slot)]

        def emit(E, sem=sem):
            E.indirect_dma_start(out=out, out_offset=out_offset, in_=in_,
                                 in_offset=in_offset, **kw).then_inc(sem, 16)
        self.prog[q].append(emit)
        self.dma_uses[q][slot] = prev_use + 1
        self.dma_n[q] += 1
        tok = ('d', q, slot, prev_use + 1)
        for k in writes:
            self.writer[k] = tok
            self.readers[k] = []
        for k in reads:
            self.readers.setdefault(k, []).append(tok)
        return tok

    def finish(self):
        for q in DMA_QUEUES:
            for s in range(NSLOT):
                use = self.dma_uses[q][s]
                if use > 0:
                    sem = self.dma_sems[(q, s)]
                    self.prog['pool'].append(
                        lambda E, sem=sem, v=use * 16: E.wait_ge(sem, v))
        for e in ENGINES:
            if e == 'pool':
                continue
            n = self.count[e]
            if n > 0:
                sem = self.sems[e]
                self.prog['pool'].append(lambda E, sem=sem, v=n: E.wait_ge(sem, v))

    def build_block(self):
        nc = self.nc
        with nc.Block() as block:
            @block.tensor
            def _(E):
                for f in self.prog['pe']:
                    f(E)

            @block.scalar
            def _(E):
                for f in self.prog['act']:
                    f(E)

            @block.vector
            def _(E):
                for f in self.prog['dve']:
                    f(E)

            @block.gpsimd
            def _(E):
                for f in self.prog['pool']:
                    f(E)

            @block.sync
            def _(E):
                for f in self.prog['sp']:
                    f(E)



def build_constants():
    C = {}
    C["ident_f32"] = np.eye(128, dtype=np.float32)
    C["ident_bf16"] = np.eye(128, dtype=BF)
    C["rev128"] = np.eye(128, dtype=np.float32)[::-1].copy().astype(BF)
    C["ones_row"] = np.ones((1, 128), np.float32)
    C["ones_row_bf"] = np.ones((1, 128), BF)
    k = np.arange(128)[:, None]; q = np.arange(128)[None, :]
    C["seg_mask"] = np.where((k // 4 == q // 4) & (k != q), 0.0, -30000.0).astype(BF)
    M = np.zeros((15, 16), np.float32); cc = np.zeros(16, np.float32)
    for l in range(16):
        bits = [(l >> (3 - i)) & 1 for i in range(4)]
        node = 0
        for bt in bits:
            M[node, l] += 2.0 * bt - 1.0
            cc[l] += 1.0 - bt
            node = 2 * node + 1 + bt
    C["treeM"] = M.astype(BF)
    C["treeC"] = cc[:, None].astype(np.float32)
    return C


CONST_SPECS = None


PSUM_BANK = {
    "dsps0": 0, "dsps1": 1, "ps_bc": 7, "mn_ps": 2,
    "f0_pt8": 2, "f0_pt8b": 3, "selps": 0,
    "f0_psh": 0, "f0_psc": 1, "f0_pso": 2,
    "l0a_psS": 3, "l0a_psL": 4, "l0a_psA": 5, "l0a_psB": 6,
    "cb_ps0": 0, "cb_ps1": 1,
}


class Ctx:
    def __init__(self, nc, S, ctxstack):
        self.nc = nc
        self.S = S
        self.es = ctxstack
        self.sb = {}
        self.ps = {}
        self.dram = {}
        self.ccols = {}
        self.uniq = 0

    def sbuf(self, name, shape, dtype):
        if name not in self.sb:
            self.sb[name] = self.es.enter_context(
                self.nc.sbuf_tensor(name, list(shape), dtype))
        return self.sb[name]

    def psum(self, name, shape, dtype=F32, bank=None):
        assert dtype == F32
        if not hasattr(self, 'psbanks'):
            self.psbanks = [self.es.enter_context(
                self.nc.psum_tensor(f"psb{i}", [128, 512], F32)) for i in range(8)]
            self.psbank_of = {}
            self._psnext = 0
        if name not in self.ps:
            if bank is None:
                bank = PSUM_BANK.get(name)
            if bank is None:
                bank = self._psnext % 8
                self._psnext += 1
            self.psbank_of[name] = bank
            self.S.alias[name] = f"PSB{bank}"
            self.ps[name] = self.psbanks[bank][0:shape[0], 0:shape[1]]
        return self.ps[name]

    def dram_scratch(self, name, shape, dtype):
        if name not in self.dram:
            kind = "ExternalOutput" if name in getattr(self, 'taps', ()) else "Internal"
            self.dram[name] = self.nc.dram_tensor(name, list(shape), dtype,
                                                  kind=kind).ap()
        return self.dram[name]

    def constcol(self, value, parts=128):
        key = (value, parts)
        if key not in self.ccols:
            t = self.sbuf(f"ccol{len(self.ccols)}", (parts, 1), F32)
            self.S.op('dve', lambda E, t=t, v=value: E.memset(t[:], float(v)),
                      writes=(f"ccol{key}",))
            self.ccols[key] = t
        return self.ccols[key]

    def uname(self, p):
        self.uniq += 1
        return f"{p}{self.uniq}"


def bcast_col(cx, scalar_ap, key_in, key_out, parts=128, scale=1.0, name=None):
    S = cx.S
    ones = cx.sbuf("ones_row_f", (1, 128), F32)
    pt = cx.psum("ps_bc", (128, 1), F32)
    out = cx.sbuf(name or cx.uname("bccol"), (parts, 1), F32)
    S.op('pe', lambda E: E.matmul(pt[0:parts, :], ones[:, 0:parts], scalar_ap,
                                  start=True, stop=True),
         reads=(key_in, "c_ones_f"), writes=("ps_bc",))
    S.op('act', lambda E: E.activation(out[:], pt[0:parts, :], AF.Copy, scale=scale),
         reads=("ps_bc",), writes=(key_out,))
    return out



def load_consts(cx, consts_aps):
    S = cx.S
    for name, shape, dtype, sbname, key in [
        ("ident_f32", (128, 128), F32, "ident_f", "c_ident_f"),
        ("ident_bf16", (128, 128), BF16, "ident_b", "c_ident_b"),
        ("rev128", (128, 128), BF16, "rev_b", "c_rev_b"),
        ("ones_row", (1, 128), F32, "ones_row_f", "c_ones_f"),
        ("ones_row_bf", (1, 128), BF16, "ones_row_b", "c_ones_b"),
        ("seg_mask", (128, 128), BF16, "seg_mask", "c_seg_mask"),
        ("treeM", (15, 16), BF16, "treeM", "c_treeM"),
        ("treeC", (16, 1), F32, "treeC", "c_treeC"),
        ("pat2", (2, 128), BF16, "pat2", "c_pat2"),
        ("pat16", (16, 1024), BF16, "pat16", "c_pat16"),
        ("trineg4", (128, 512), BF16, "trineg4", "c_trineg4"),
        ("trilo", (128, 128), F32, "trilo", "c_trilo"),
        ("triuneg", (128, 128), BF16, "triuneg", "c_triuneg"),
        ("iota128", (128, 1), F32, "iota128", "c_iota"),
    ]:
        t = cx.sbuf(sbname, shape, dtype)
        S.dma('sp', t[:], consts_aps[name], writes=(key,))



def emit_layernorm(cx, x_dram, xn_dram, xnlo_dram, li, keep_stats=False):
    S = cx.S
    nc = cx.nc
    xt = [cx.sbuf("ln_x0", (128, 1024), F32)] * 2
    mcol = cx.sbuf("ln_m", (128, 1), F32)
    scol = cx.sbuf("ln_s", (128, 1), F32)
    vcol = cx.sbuf("ln_v", (128, 1), F32)
    rcol = cx.sbuf("ln_r", (128, 1), F32)
    xnt = cx.sbuf("ln_xn", (128, 1024), F32)
    xnb = cx.sbuf("ln_xnb", (128, 1024), BF16)
    xlo = cx.sbuf("ln_xlo", (128, 1024), BF16)
    stats = cx.sbuf("lnstats", (128, 2 * NTCH), F32) if keep_stats else None
    epscol = cx.constcol(1e-5)
    for c in range(NTCH):
        xb = xt[0]
        kx = "ln_x0"
        S.dma('sp', xb[:], x_dram[c * 128:(c + 1) * 128, :], writes=(kx,))
        S.op('act', lambda E, xb=xb: E.activation(xnt[:], xb[:], AF.Square),
             reads=(kx,), writes=("ln_xn",))
        S.op('dve', lambda E: E.reduce_sum(scol[:], xnt[:], axis=AX.X),
             reads=("ln_xn",), writes=("ln_s",))
        S.op('dve', lambda E, xb=xb: E.reduce_sum(mcol[:], xb[:], axis=AX.X),
             reads=(kx,), writes=("ln_m",))
        S.op('dve', lambda E: E.tensor_single_scalar(mcol[:], mcol[:], 1.0 / D, ALU.mult),
             reads=("ln_m",), writes=("ln_m",))
        S.op('dve', lambda E: E.scalar_tensor_tensor(vcol[:], mcol[:], -1.0, mcol[:],
                                                     ALU.mult, ALU.mult),
             reads=("ln_m",), writes=("ln_v",))
        S.op('dve', lambda E: E.scalar_tensor_tensor(vcol[:], scol[:], 1.0 / D, vcol[:],
                                                     ALU.mult, ALU.add),
             reads=("ln_s", "ln_v"), writes=("ln_v",))
        S.op('act', lambda E: E.activation(rcol[:], vcol[:], AF.Ln,
                                           bias=epscol[:]),
             reads=("ln_v", f"ccol{(1e-05, 128)}"), writes=("ln_r",))
        S.op('act', lambda E: E.activation(rcol[:], rcol[:], AF.Exp, scale=-0.5),
             reads=("ln_r",), writes=("ln_r",))
        if keep_stats:
            S.op('dve', lambda E, c=c: E.tensor_copy(stats[:, 2 * c:2 * c + 1], mcol[:]),
                 reads=("ln_m",), writes=("lnstats",))
            S.op('dve', lambda E, c=c: E.tensor_copy(stats[:, 2 * c + 1:2 * c + 2], rcol[:]),
                 reads=("ln_r",), writes=("lnstats",))
        S.op('dve', lambda E, xb=xb: E.tensor_scalar(xnt[:], xb[:], mcol[:], rcol[:],
                                                     ALU.subtract, ALU.mult),
             reads=(kx, "ln_m", "ln_r"), writes=("ln_xn",))
        S.op('dve', lambda E: E.tensor_copy(xnb[:], xnt[:]),
             reads=("ln_xn",), writes=("ln_xnb",))
        S.op('dve', lambda E: E.tensor_sub(xlo[:], xnt[:], xnb[:]),
             reads=("ln_xn", "ln_xnb"), writes=("ln_xlo",))
        S.dma('sp', xn_dram[c * 128:(c + 1) * 128, :], xnb[:], reads=("ln_xnb",))
        S.dma('sp', xnlo_dram[c * 128:(c + 1) * 128, :], xlo[:], reads=("ln_xlo",))



def emit_combine2(cx, x_dram, branch_drams, xout_dram, extra_tm=None):
    S = cx.S
    ta = cx.sbuf("cb_a", (128, 128), F32)
    ts = cx.sbuf("cb_s", (128, 128), F32)
    pt = [cx.psum(f"cb_ps{i}", (128, 128), F32) for i in range(2)]
    ot = cx.sbuf("cb_o", (128, 1024), F32)
    te = cx.sbuf("cb_e", (128, 1024), F32) if extra_tm is not None else None
    idf = cx.sb["ident_f"]
    for tc in range(NTCH):
        t0 = tc * 128
        S.dma('sp', ot[:], x_dram[t0:t0 + 128, :], writes=("cb_o",))
        if extra_tm is not None:
            S.dma('sp', te[:], extra_tm[t0:t0 + 128, :], writes=("cb_e",))
            S.op('dve', lambda E: E.tensor_add(ot[:], ot[:], te[:]),
                 reads=("cb_e", "cb_o"), writes=("cb_o",))
        for dc in range(NDCH):
            for bi, bd in enumerate(branch_drams):
                if bi == 0:
                    S.dma('sp', ts[:], bd[dc * 128:(dc + 1) * 128, t0:t0 + 128],
                          writes=("cb_s",))
                else:
                    S.dma('sp', ta[:], bd[dc * 128:(dc + 1) * 128, t0:t0 + 128],
                          writes=("cb_a",))
                    S.op('dve', lambda E: E.tensor_add(ts[:], ts[:], ta[:]),
                         reads=("cb_a", "cb_s"), writes=("cb_s",))
            p = pt[dc % 2]
            kp = f"cb_ps{dc % 2}"
            S.op('pe', lambda E, p=p: E.transpose(p[:], ts[:], idf[:]),
                 reads=("cb_s", "c_ident_f"), writes=(kp,))
            S.op('dve', lambda E, dc=dc, p=p: E.tensor_add(
                    ot[:, dc * 128:(dc + 1) * 128],
                    ot[:, dc * 128:(dc + 1) * 128], p[:]),
                 reads=(kp, "cb_o"), writes=("cb_o",))
        S.dma('sp', xout_dram[t0:t0 + 128, :], ot[:], reads=("cb_o",))



class WPool:
    def __init__(self, cx, name, nchunks, cw, dtype=BF16):
        self.cx, self.name, self.cw = cx, name, cw
        self.n = nchunks
        self.buf = cx.sbuf(name, (128, nchunks * cw), dtype)
        self.ptr = 0

    def reset(self):
        self.ptr = 0

    def alloc(self, nch=1):
        assert self.ptr + nch <= self.n, f"{self.name} pool overflow"
        j = self.ptr
        self.ptr += nch
        keys = tuple(f"{self.name}_c{i}" for i in range(j, j + nch))
        return self.buf[:, j * self.cw:(j + nch) * self.cw], keys

    def allocf32(self, nch=1):
        ap, keys = self.alloc(nch)
        return ap.bitcast(F32), keys


class FmSrc:
    def __init__(self, cx, name, tm_dram, pool, n_ic=NDCH, tok_w=512):
        self.cx, self.name, self.tm = cx, name, tm_dram
        self.tok_w = tok_w
        self.tiles = [pool.alloc(1) for _ in range(n_ic)]
        self.cur = [None] * n_ic

    def get(self, ic, tt):
        ap, keys = self.tiles[ic]
        if self.cur[ic] != tt:
            t0 = tt * self.tok_w
            self.cx.S.dma('sp', ap,
                          self.tm[t0:t0 + self.tok_w, ic * 128:(ic + 1) * 128],
                          writes=keys, transpose=True)
            self.cur[ic] = tt
        return ap, keys[0]


class FmDirect:
    def __init__(self, cx, name, fm_dram, pool, n_ic=NDCH, tok_w=512):
        self.cx, self.name, self.fm = cx, name, fm_dram
        self.tok_w = tok_w
        self.tiles = [pool.alloc(1) for _ in range(n_ic)]
        self.cur = [None] * n_ic

    def get(self, ic, tt):
        ap, keys = self.tiles[ic]
        if self.cur[ic] != tt:
            self.cx.S.dma('sp', ap,
                          self.fm[ic * 128:(ic + 1) * 128,
                                  tt * self.tok_w:(tt + 1) * self.tok_w],
                          writes=keys)
            self.cur[ic] = tt
        return ap, keys[0]


class WTorch:
    def __init__(self, cx, name, wbf_dram, n_in, n_out, pool):
        self.cx, self.name = cx, name
        self.wbf = wbf_dram
        self.nic, self.noc = n_in // 128, n_out // 128
        self.tiles = {}
        self.pool = pool

    def get(self, ic, oc):
        if (ic, oc) not in self.tiles:
            ap, keys = self.pool.alloc(1)
            rows = min(128, self.wbf.shape[0] - oc * 128)
            self.cx.S.dma('sp', ap[:, 0:rows],
                          self.wbf[oc * 128:oc * 128 + rows, ic * 128:(ic + 1) * 128],
                          writes=keys, transpose=True)
            self.tiles[(ic, oc)] = (ap, keys[0])
        return self.tiles[(ic, oc)]


class WJaxBF:
    def __init__(self, cx, name, w_dram, n_in, n_out, pool):
        self.cx, self.name = cx, name
        self.w = w_dram
        self.nic = n_in // 128
        self.n_out = n_out
        self.rows = {}
        self.pool = pool

    def get(self, ic, oc, ow=128):
        if ic not in self.rows:
            nch = (self.n_out + 127) // 128
            ap, keys = self.pool.alloc(nch)
            self.cx.S.dma('sp', ap[:, 0:self.n_out],
                          self.w[ic * 128:(ic + 1) * 128, :], writes=keys)
            self.rows[ic] = (ap, keys)
        ap, keys = self.rows[ic]
        return ap[:, oc * ow:oc * ow + ow], keys[min(oc * ow // 128, len(keys) - 1)]


class WJaxWin:
    def __init__(self, cx, name, w_dram, n_in, col0, ncols, pool):
        self.cx, self.name = cx, name
        self.w = w_dram
        self.col0, self.ncols = col0, ncols
        self.rows = {}
        self.pool = pool

    def get(self, ic, oc, ow=128):
        if ic not in self.rows:
            nch = (self.ncols + 127) // 128
            ap, keys = self.pool.alloc(nch)
            self.cx.S.dma('pool', ap[:, 0:self.ncols],
                          self.w[ic * 128:(ic + 1) * 128,
                                 self.col0:self.col0 + self.ncols], writes=keys)
            self.rows[ic] = (ap, keys)
        ap, keys = self.rows[ic]
        off = oc * ow - self.col0
        return ap[:, off:off + ow], keys[min(off // 128, len(keys) - 1)]


class WJax:
    def __init__(self, cx, name, w_dram, n_in, n_out, pool):
        self.cx, self.name = cx, name
        self.w = w_dram
        self.nic = n_in // 128
        self.n_out = n_out
        self.rows = {}
        self.pool = pool

    def get(self, ic, oc, ow=128):
        if ic not in self.rows:
            nch = (self.n_out + 127) // 128
            ap, keys = self.pool.alloc(nch)
            self.cx.S.dma('pool', ap[:, 0:self.n_out],
                          self.w[ic * 128:(ic + 1) * 128, :], writes=keys)
            self.rows[ic] = (ap, keys)
        ap, keys = self.rows[ic]
        return ap[:, oc * ow:oc * ow + ow], keys[min(oc * ow // 128, len(keys) - 1)]


def emit_cast_bf16(cx, src_f32, dst_bf16, rows, cols, tag):
    S = cx.S
    buf = cx.sbuf("castbuf", (128, 1024), BF16)
    r = 0
    i = 0
    while r < rows:
        rr = min(128, rows - r)
        c = 0
        while c < cols:
            ww = min(1024, cols - c)
            key = f"cast_{tag}_{i}"
            S.dma('pool', buf[0:rr, 0:ww], src_f32[r:r + rr, c:c + ww], writes=("castbuf",))
            S.dma('sp', dst_bf16[r:r + rr, c:c + ww], buf[0:rr, 0:ww], reads=("castbuf",))
            c += ww
            i += 1
        r += 128


def load_bias_cols(cx, name, b_dram, n):
    t = cx.sbuf(f"{name}_bias", (128, n // 128), F32)
    for j in range(n // 128):
        cx.S.dma('sp', t[:, j:j + 1], b_dram[j * 128:(j + 1) * 128].unsqueeze(1)
                 if b_dram.ndim == 1 else b_dram[j * 128:(j + 1) * 128, :],
                 writes=(f"{name}_bias",))
    return t


def emit_dense_fm(cx, tag, wget, src, n_ic, n_oc, out_sink, n_tt=8, tok_w=512,
                  extra_mm=None):
    S = cx.S
    ps = [cx.psum(f"dsps{i}", (128, 512), F32) for i in range(2)]
    for tt in range(n_tt):
        for oc in range(n_oc):
            p = ps[oc % 2]
            pskey = f"dsps{oc % 2}"
            mms = []
            for ic in range(n_ic):
                w_ap, wkey = wget(ic, oc)
                s_ap, skey = src(ic, tt)
                mms.append((w_ap, wkey, s_ap, skey))

            def fn(E, mms=mms, p=p, tok_w=tok_w, oc=oc, tt=tt):
                inst = None
                for i, (w_ap, wk, s_ap, sk) in enumerate(mms):
                    inst = E.matmul(p[:, 0:tok_w], w_ap, s_ap,
                                    start=(i == 0), stop=(i == len(mms) - 1 and extra_mm is None))
                if extra_mm is not None:
                    inst = extra_mm(E, p, oc, tt)
                return inst
            reads = tuple(set([k for (_, k, _, _) in mms] + [k for (_, _, _, k) in mms]))
            S.op('pe', fn, reads=reads, writes=(pskey,))
            out_sink(p, oc, tt, pskey)


def sink_to_dram(cx, tag, out_dram, dtype=BF16, bias=None, act=None, scale=1.0,
                 tok_w=512, pool=None):
    S = cx.S
    if pool is None:
        bufs = [cx.sbuf(f"{tag}_o{i}", (128, tok_w), dtype) for i in range(2)]
        keys = [f"{tag}_o{i}" for i in range(2)]
    else:
        bufs, keys = [], []
        for i in range(2):
            if dtype == F32:
                b_, k_ = pool.allocf32(tok_w // 256)
            else:
                b_, k_ = pool.alloc(tok_w // 512)
            bufs.append(b_[:, 0:tok_w])
            keys.append(k_[0])

    def sink(p, oc, tt, pskey):
        b = bufs[oc % 2]
        key = keys[oc % 2]
        kw = {}
        reads = [pskey]
        fn_ = act
        if bias is not None:
            bt, bkey = bias
            kw['bias'] = bt[:, oc:oc + 1]
            reads.append(bkey)
            if fn_ is None:
                fn_ = AF.Identity
        elif fn_ is None:
            fn_ = AF.Copy
        S.op('act', lambda E, p=p, b=b, kw=kw, fn_=fn_: E.activation(
                b[:, 0:tok_w], p[:, 0:tok_w], fn_, scale=scale, **kw),
             reads=tuple(reads), writes=(key,))
        S.dma('sp', out_dram[oc * 128:(oc + 1) * 128, tt * tok_w:(tt + 1) * tok_w],
              b[:, 0:tok_w], reads=(key,))
    return sink



def emit_maxnorm2(cx, tag, fm_dram, row0, nrows, ncols=S_LEN, nchunks=1):
    S = cx.S
    buf = cx.sbuf("mn_in", (128, 512), BF16)
    sq = cx.sbuf("mn_sq", (128, 512), F32)
    pn = cx.psum("mn_ps", (1, 512), F32)
    mrow = cx.sbuf("mn_row", (1, 8), F32)
    out = cx.sbuf(cx.uname("mn_out"), (1, 1), F32)
    okey = f"mn_out_{tag}"
    onesc = cx.sbuf("ones_col_f", (128, 1), F32)
    nr = nrows
    for sl in range(ncols // 512):
        for rc in range(nchunks):
            r0 = row0 + rc * 128
            S.dma('sp', buf[0:nr, :], fm_dram[r0:r0 + nr, sl * 512:(sl + 1) * 512],
                  writes=("mn_in",))
            S.op('act', lambda E: E.activation(sq[0:nr, :], buf[0:nr, :], AF.Square),
                 reads=("mn_in",), writes=("mn_sq",))
            S.op('pe', lambda E, rc=rc: E.matmul(pn[:], onesc[0:nr, :], sq[0:nr, :],
                                                 start=(rc == 0),
                                                 stop=(rc == nchunks - 1)),
                 reads=("mn_sq", "c_ones_col"), writes=("mn_ps",))
        S.op('dve', lambda E, sl=sl: E.reduce_max(mrow[:, sl:sl + 1], pn[:], axis=AX.X),
             reads=("mn_ps",), writes=("mn_row",))
    S.op('dve', lambda E: E.reduce_max(out[:], mrow[:], axis=AX.X),
         reads=("mn_row",), writes=(okey,))
    return out, okey


def emit_scalar_sqrtmul(cx, a, akey, b, bkey, tag, slack=0.0):
    S = cx.S
    ra = cx.sbuf(cx.uname("ssq"), (1, 2), F32)
    out = cx.sbuf(cx.uname("ssm"), (1, 1), F32)
    okey = f"ssm_{tag}"
    S.op('dve', lambda E: E.tensor_copy(ra[:, 0:1], a[:]), reads=(akey,), writes=(f"ssq_{tag}",))
    S.op('dve', lambda E: E.tensor_copy(ra[:, 1:2], b[:]), reads=(bkey,), writes=(f"ssq_{tag}",))
    S.op('act', lambda E: E.activation(ra[:], ra[:], AF.Sqrt),
         reads=(f"ssq_{tag}",), writes=(f"ssq_{tag}",))
    S.op('dve', lambda E: E.tensor_tensor(out[:], ra[:, 0:1], ra[:, 1:2], ALU.mult),
         reads=(f"ssq_{tag}",), writes=(okey,))
    if slack:
        S.op('dve', lambda E: E.tensor_single_scalar(out[:], out[:], float(slack), ALU.add),
             reads=(okey,), writes=(okey,))
    return out, okey



def emit_L0_attn(cx, W, xn_tm, battn_dram):
    S = cx.S
    wp = cx.wpool
    sp = cx.srcpool
    HD = 128
    H = 8
    for nm, (r, c) in [("qk", (1024, 1024)), ("v", (1024, 1024)),
                       ("inp", (3072, 1024)), ("out", (1024, 1024))]:
        dst = cx.dram_scratch(f"l0w_{nm}", (r, c), BF16)
        emit_cast_bf16(cx, W[f"l0_{nm}_f32"], dst, r, c, f"l0{nm}")
    qkin = cx.dram_scratch("l0_qkin", (D, S_LEN), BF16)
    vin = cx.dram_scratch("l0_vin", (D, S_LEN), BF16)
    Q = cx.dram_scratch("l0_Q", (D, S_LEN), BF16)
    K = cx.dram_scratch("l0_K", (D, S_LEN), BF16)
    Vfm = cx.dram_scratch("l0_Vfm", (D, S_LEN), BF16)
    AO = cx.dram_scratch("l0_AO", (D, S_LEN), BF16)

    bqk = load_bias_cols(cx, "l0bqk", W["l0_qkb"], 1024)
    bv = load_bias_cols(cx, "l0bv", W["l0_vb"], 1024)
    binp = load_bias_cols(cx, "l0binp", W["l0_inpb"], 3072)
    bout = load_bias_cols(cx, "l0bout", W["l0_outb"], 1024)

    sp.reset(); wp.reset()
    xnsrc = FmSrc(cx, "l0xn", xn_tm, sp)
    wqk = WTorch(cx, "l0qk", cx.dram["l0w_qk"], 1024, 1024, wp)
    emit_dense_fm(cx, "l0qkin", wqk.get, xnsrc.get, NDCH, NDCH,
                  sink_to_dram(cx, "l0qkin", qkin, bias=(bqk, "l0bqk_bias")))
    sp.reset(); wp.reset()
    xnsrc = FmSrc(cx, "l0xn2", xn_tm, sp)
    wv = WTorch(cx, "l0v", cx.dram["l0w_v"], 1024, 1024, wp)
    emit_dense_fm(cx, "l0vin", wv.get, xnsrc.get, NDCH, NDCH,
                  sink_to_dram(cx, "l0vin", vin, bias=(bv, "l0bv_bias")))
    sp.reset(); wp.reset()
    qkinsrc = FmDirect(cx, "l0qkd", qkin, sp)
    winpq = WTorch(cx, "l0inpq", cx.dram["l0w_inp"], 1024, 1024, wp)
    emit_dense_fm(cx, "l0q", winpq.get, qkinsrc.get, NDCH, NDCH,
                  sink_to_dram(cx, "l0q", Q, bias=(binp, "l0binp_bias")))
    sp.reset(); wp.reset()
    qkinsrc = FmDirect(cx, "l0qkd2", qkin, sp)
    winpk = WTorch(cx, "l0inpk", cx.dram["l0w_inp"][1024:2048, :], 1024, 1024, wp)
    emit_dense_fm(cx, "l0k", winpk.get, qkinsrc.get, NDCH, NDCH,
                  sink_sel_bias(cx, "l0k", K, binp, "l0binp_bias", 8))
    sp.reset(); wp.reset()
    vinsrc = FmDirect(cx, "l0vnd", vin, sp)
    winpv = WTorch(cx, "l0inpv", cx.dram["l0w_inp"][2048:3072, :], 1024, 1024, wp)
    emit_dense_fm(cx, "l0v2", winpv.get, vinsrc.get, NDCH, NDCH,
                  sink_sel_bias(cx, "l0v2", Vfm, binp, "l0binp_bias", 16))
    onesb = cx.sbuf("ones_col_b", (128, 1), BF16)
    onesc = cx.sbuf("ones_col_f", (128, 1), F32)
    qscale = 1.0 / math.sqrt(HD)
    qt = cx.sbuf("l0a_q", (128, 128), BF16)
    kt = cx.sbuf("l0a_k", (128, 128), BF16)
    vt = cx.sbuf("l0a_v", (128, 128), BF16)
    pT = cx.sbuf("l0a_p", (128, 128), BF16)
    lr = cx.sbuf("l0a_l", (1, 128), F32)
    rb = cx.sbuf("l0a_rb", (128, 128), F32)
    ob = cx.sbuf("l0a_ob", (128, 128), BF16)
    psS = cx.psum("l0a_psS", (128, 128))
    psL = cx.psum("l0a_psL", (1, 128))
    psA = cx.psum("l0a_psA", (128, 128))
    psB = cx.psum("l0a_psB", (128, 128))
    segm = cx.sb["seg_mask"]
    identb = cx.sb["ident_b"]
    for h in range(H):
        mq, mqk_ = emit_maxnorm2(cx, f"l0q{h}", Q, h * 128, 128)
        mk, mkk_ = emit_maxnorm2(cx, f"l0k{h}", K, h * 128, 128)
        c, ckey = emit_scalar_sqrtmul(cx, mq, mqk_, mk, mkk_, f"l0c{h}")
        bc = bcast_col(cx, c[:], ckey, f"l0bc{h}", scale=-qscale)
        for tc in range(NTCH):
            t0 = tc * 128
            S.dma('sp', qt[:], Q[h * 128:(h + 1) * 128, t0:t0 + 128], writes=("l0a_q",))
            S.dma('sp', kt[:], K[h * 128:(h + 1) * 128, t0:t0 + 128], writes=("l0a_k",))
            S.dma('sp', vt[:], Vfm[h * 128:(h + 1) * 128, t0:t0 + 128],
                  writes=("l0a_v",), transpose=True)

            def mmS(E):
                E.matmul(psS[:], kt[:], qt[:], start=True, stop=False)
                return E.matmul(psS[:], identb[:], segm[:], start=False, stop=True)
            S.op('pe', mmS, reads=("l0a_q", "l0a_k", "c_ident_b", "c_seg_mask"),
                 writes=("l0a_psS",))
            S.op('act', lambda E, bc=bc: E.activation(pT[:], psS[:], AF.Exp,
                                                      scale=qscale, bias=bc[:]),
                 reads=("l0a_psS", f"l0bc{h}"), writes=("l0a_p",))
            S.op('pe', lambda E: E.matmul(psL[:], onesb[:], pT[:], start=True, stop=True),
                 reads=("l0a_p", "c_ones_col_b"), writes=("l0a_psL",))
            S.op('pe', lambda E: E.matmul(psA[:], vt[:], pT[:], start=True, stop=True),
                 reads=("l0a_p", "l0a_v"), writes=("l0a_psA",))
            S.op('dve', lambda E: E.reciprocal(lr[:], psL[:]),
                 reads=("l0a_psL",), writes=("l0a_l",))
            S.op('pe', lambda E: E.matmul(psB[:], cx.sb["ones_row_f"][:], lr[:],
                                          start=True, stop=True),
                 reads=("l0a_l", "c_ones_f"), writes=("l0a_psB",))
            S.op('act', lambda E: E.activation(rb[:], psB[:], AF.Copy),
                 reads=("l0a_psB",), writes=("l0a_rb",))
            S.op('dve', lambda E: E.tensor_mul(ob[:], psA[:], rb[:]),
                 reads=("l0a_psA", "l0a_rb"), writes=("l0a_ob",))
            S.dma('sp', AO[h * 128:(h + 1) * 128, t0:t0 + 128], ob[:], reads=("l0a_ob",))
    sp.reset(); wp.reset()
    aosrc = FmDirect(cx, "l0ao", AO, sp)
    wout = WTorch(cx, "l0out", cx.dram["l0w_out"], 1024, 1024, wp)
    emit_dense_fm(cx, "l0o", wout.get, aosrc.get, NDCH, NDCH,
                  sink_to_dram(cx, "l0o", battn_dram, dtype=F32,
                               bias=(bout, "l0bout_bias")))


def sink_sel_bias(cx, tag, out_dram, bias_t, bias_key, bias_off, tok_w=512):
    S = cx.S
    bufs = [cx.sbuf(f"{tag}_o{i}", (128, tok_w), BF16) for i in range(2)]

    def sink(p, oc, tt, pskey):
        b = bufs[oc % 2]
        key = f"{tag}_o{oc % 2}"
        S.op('act', lambda E, p=p, b=b, oc=oc: E.activation(
                b[:, 0:tok_w], p[:, 0:tok_w], AF.Identity,
                bias=bias_t[:, bias_off + oc:bias_off + oc + 1]),
             reads=(pskey, bias_key), writes=(key,))
        S.dma('sp', out_dram[oc * 128:(oc + 1) * 128, tt * tok_w:(tt + 1) * tok_w],
              b[:, 0:tok_w], reads=(key,))
    return sink



def emit_sel_logits(cx, tag, xn_hi_src, xn_lo_src, w_dram, nsel):
    S = cx.S
    whi = cx.sbuf(f"{tag}_whi", (128, NDCH * nsel), BF16)
    wlo = cx.sbuf(f"{tag}_wlo", (128, NDCH * nsel), BF16)
    wf = cx.sbuf(f"{tag}_wf", (128, nsel), F32)
    wfb = cx.sbuf(f"{tag}_wfb", (128, nsel), F32)
    out = cx.dram_scratch(f"{tag}_lg", (nsel, S_LEN), F32)
    okey = "D_" + f"{tag}_lg"
    lgb = cx.sbuf("sel_lgb", (nsel, 512), F32)
    pg = cx.psum("selps", (nsel, 512))
    for ic in range(NDCH):
        kwf = f"{tag}_wf"
        S.dma('sp', wf[:], w_dram[ic * 128:(ic + 1) * 128, :], writes=(kwf,))
        S.op('dve', lambda E, ic=ic: E.tensor_copy(whi[:, ic * nsel:(ic + 1) * nsel], wf[:]),
             reads=(kwf,), writes=(f"{tag}_whi",))
        S.op('dve', lambda E, ic=ic: E.tensor_copy(wfb[:], whi[:, ic * nsel:(ic + 1) * nsel]),
             reads=(f"{tag}_whi",), writes=(f"{tag}_wfb",))
        S.op('dve', lambda E, ic=ic: E.tensor_sub(wfb[:], wf[:], wfb[:]),
             reads=(kwf, f"{tag}_wfb"), writes=(f"{tag}_wfb",))
        S.op('dve', lambda E, ic=ic: E.tensor_copy(wlo[:, ic * nsel:(ic + 1) * nsel], wfb[:]),
             reads=(f"{tag}_wfb",), writes=(f"{tag}_wlo",))
    for tt in range(8):
        mms = []
        for ic in range(NDCH):
            xh, kh = xn_hi_src(ic, tt)
            xl, kl = xn_lo_src(ic, tt)
            mms.append((whi[:, ic * nsel:(ic + 1) * nsel], f"{tag}_whi", xh, kh))
            mms.append((whi[:, ic * nsel:(ic + 1) * nsel], f"{tag}_whi", xl, kl))
            mms.append((wlo[:, ic * nsel:(ic + 1) * nsel], f"{tag}_wlo", xh, kh))

        def fn(E, mms=mms):
            inst = None
            for i, (w, wk, x, xk) in enumerate(mms):
                inst = E.matmul(pg[0:nsel, :], w, x, start=(i == 0),
                                stop=(i == len(mms) - 1))
            return inst
        reads = tuple(set([k for (_, k, _, _) in mms] + [k for (_, _, _, k) in mms]))
        S.op('pe', fn, reads=reads, writes=("selps",))
        S.op('dve', lambda E, tt=tt: E.tensor_copy(lgb[:], pg[0:nsel, :]),
             reads=("selps",), writes=("sel_lgb",))
        S.dma('sp', out[:, tt * 512:(tt + 1) * 512], lgb[:], reads=("sel_lgb",))
    return out, okey



def emit_firstone(cx, tag, g_t, gkey, out_t, okey, n, zeros_t, zkey):
    S = cx.S
    cm = cx.sbuf(f"{tag}_cm", (128, n), F32)
    S.op('dve', lambda E: E.tensor_tensor_scan(cm[:], g_t, zeros_t, 0.0,
                                               ALU.max, ALU.add),
         reads=(gkey, zkey), writes=(f"{tag}_cm",))
    S.op('dve', lambda E: E.tensor_sub(out_t[:, 1:n], cm[:, 1:n], cm[:, 0:n - 1]),
         reads=(f"{tag}_cm",), writes=(okey,))
    S.op('dve', lambda E: E.tensor_copy(out_t[:, 0:1], cm[:, 0:1]),
         reads=(f"{tag}_cm",), writes=(okey,))



def emit_L0_ffn(cx, W, xn_tm, xnlo_tm, bffn_dram):
    S = cx.S
    cx.srcpool.reset(); cx.wpool.reset()
    xnsrc = FmSrc(cx, "f0xn", xn_tm, cx.srcpool)
    xnlos = FmSrc(cx, "f0xl", xnlo_tm, cx.srcpool)
    lg, lgkey = emit_sel_logits(cx, "f0g", xnsrc.get, xnlos.get, W["l0_Wg"], 8)
    cw_fm = cx.sbuf("f0_cw", (8, S_LEN), BF16)
    lgs = cx.sbuf("f0_lgs", (8, 128), F32)
    lt = cx.sbuf("f0_lt", (128, 8), F32)
    zeros8 = cx.sbuf("f0_z8", (128, 8), F32)
    m1 = cx.sbuf("f0_m1", (128, 1), F32)
    m2 = cx.sbuf("f0_m2", (128, 1), F32)
    g1 = cx.sbuf("f0_g1", (128, 8), F32)
    f1 = cx.sbuf("f0_f1", (128, 8), F32)
    f2 = cx.sbuf("f0_f2", (128, 8), F32)
    msk = cx.sbuf("f0_msk", (128, 8), F32)
    w1 = cx.sbuf("f0_w1", (128, 1), F32)
    w2 = cx.sbuf("f0_w2", (128, 1), F32)
    cwt = cx.sbuf("f0_cwt", (128, 8), F32)
    pt8 = cx.psum("f0_pt8", (128, 8), F32)
    pt8b = cx.psum("f0_pt8b", (8, 128), F32)
    idf = cx.sb["ident_f"]
    S.op('dve', lambda E: E.memset(zeros8[:], 0.0), writes=("f0_z8",))
    for c in range(NTCH):
        S.dma('sp', lgs[:], lg[:, c * 128:(c + 1) * 128], writes=("f0_lgs",))
        S.op('pe', lambda E: E.transpose(pt8[:], lgs[:], idf[0:8, 0:8]),
             reads=("f0_lgs", "c_ident_f"), writes=("f0_pt8",))
        S.op('dve', lambda E: E.tensor_copy(lt[:], pt8[:]),
             reads=("f0_pt8",), writes=("f0_lt",))
        S.op('dve', lambda E: E.reduce_max(m1[:], lt[:], axis=AX.X),
             reads=("f0_lt",), writes=("f0_m1",))
        S.op('dve', lambda E: E.tensor_scalar(g1[:], lt[:], m1[:], None, ALU.is_ge),
             reads=("f0_lt", "f0_m1"), writes=("f0_g1",))
        emit_firstone(cx, "f0a", g1[:], "f0_g1", f1, "f0_f1", 8, zeros8[:], "f0_z8")
        S.op('dve', lambda E: E.scalar_tensor_tensor(msk[:], f1[:], -30000.0, lt[:],
                                                     ALU.mult, ALU.add),
             reads=("f0_f1", "f0_lt"), writes=("f0_msk",))
        S.op('dve', lambda E: E.reduce_max(m2[:], msk[:], axis=AX.X),
             reads=("f0_msk",), writes=("f0_m2",))
        S.op('dve', lambda E: E.tensor_scalar(g1[:], msk[:], m2[:], None, ALU.is_ge),
             reads=("f0_msk", "f0_m2"), writes=("f0_g1",))
        emit_firstone(cx, "f0b", g1[:], "f0_g1", f2, "f0_f2", 8, zeros8[:], "f0_z8")
        S.op('dve', lambda E: E.tensor_sub(w1[:], m1[:], m2[:]),
             reads=("f0_m1", "f0_m2"), writes=("f0_w1",))
        S.op('act', lambda E: E.activation(w1[:], w1[:], AF.Sigmoid),
             reads=("f0_w1",), writes=("f0_w1",))
        S.op('dve', lambda E: E.tensor_scalar(w2[:], w1[:], -1.0, 1.0, ALU.mult, ALU.add),
             reads=("f0_w1",), writes=("f0_w2",))
        S.op('dve', lambda E: E.tensor_scalar(cwt[:], f1[:], w1[:], None, ALU.mult),
             reads=("f0_f1", "f0_w1"), writes=("f0_cwt",))
        S.op('dve', lambda E: E.tensor_scalar(f2[:], f2[:], w2[:], None, ALU.mult),
             reads=("f0_f2", "f0_w2"), writes=("f0_f2",))
        S.op('dve', lambda E: E.tensor_add(cwt[:], cwt[:], f2[:]),
             reads=("f0_cwt", "f0_f2"), writes=("f0_cwt",))
        S.op('pe', lambda E: E.transpose(pt8b[:], cwt[:], idf[:]),
             reads=("f0_cwt", "c_ident_f"), writes=("f0_pt8b",))
        S.op('act', lambda E, c=c: E.activation(cw_fm[:, c * 128:(c + 1) * 128],
                                                pt8b[:], AF.Copy),
             reads=("f0_pt8b",), writes=("f0_cw",))
    aT = cx.sbuf("f0_aT", (128, NDCH * 128), BF16)
    for ic in range(NDCH):
        for e in range(8):
            S.dma('pool', aT[:, ic * 128 + e * 16: ic * 128 + (e + 1) * 16],
                  W["l0_A"][e, ic * 128:(ic + 1) * 128, :], writes=("f0_aT",))
    bT = cx.sbuf("f0_bT", (128, 1024), BF16)
    S.dma('pool', bT[:], W["l0_B"].rearrange("e r d -> (e r) d"), writes=("f0_bT",))
    p8 = cx.sbuf("f0_p8", (8, 128), BF16)
    pat = np.zeros((8, 128), BF); pat[np.arange(128) // 16, np.arange(128)] = 1
    S.dma('sp', p8[:], cx.consts["pat8"], writes=("f0_p8",))
    hw = cx.sbuf("f0_hw", (128, 512), BF16)
    cwr = cx.sbuf("f0_cwr", (128, 512), BF16)
    psh = cx.psum("f0_psh", (128, 512), F32)
    psc = cx.psum("f0_psc", (128, 512), F32)
    pso = cx.psum("f0_pso", (128, 512), F32)
    ob = cx.sbuf("f0_ob", (128, 512), F32)
    for tt in range(8):
        mms = []
        for ic in range(NDCH):
            x_ap, xk = xnsrc.get(ic, tt)
            mms.append((aT[:, ic * 128:(ic + 1) * 128], "f0_aT", x_ap, xk))

        def fnh(E, mms=mms):
            inst = None
            for i, (w, wk, x, xk) in enumerate(mms):
                inst = E.matmul(psh[:], w, x, start=(i == 0), stop=(i == len(mms) - 1))
            return inst
        S.op('pe', fnh, reads=tuple(set(k for m in mms for k in (m[1], m[3]))),
             writes=("f0_psh",))
        S.op('pe', lambda E, tt=tt: E.matmul(psc[:], p8[:], cw_fm[:, tt * 512:(tt + 1) * 512],
                                             start=True, stop=True),
             reads=("f0_p8", "f0_cw"), writes=("f0_psc",))
        S.op('act', lambda E: E.activation(cwr[:], psc[:], AF.Copy),
             reads=("f0_psc",), writes=("f0_cwr",))
        S.op('dve', lambda E: E.tensor_mul(hw[:], psh[:], cwr[:]),
             reads=("f0_psh", "f0_cwr"), writes=("f0_hw",))
        for oc in range(NDCH):
            S.op('pe', lambda E, oc=oc: E.matmul(pso[:], bT[:, oc * 128:(oc + 1) * 128],
                                                 hw[:], start=True, stop=True),
                 reads=("f0_bT", "f0_hw"), writes=("f0_pso",))
            S.op('act', lambda E: E.activation(ob[:], pso[:], AF.Copy),
                 reads=("f0_pso",), writes=("f0_ob",))
            S.dma('sp', bffn_dram[oc * 128:(oc + 1) * 128, tt * 512:(tt + 1) * 512],
                  ob[:], reads=("f0_ob",))



import contextlib

L1_WEIGHT_NAMES = {
    "l1_wq": "L1_attn_wq", "l1_wk": "L1_attn_wk", "l1_wv": "L1_attn_wv",
    "l1_wo": "L1_attn_wo", "l1_rel_emb": "L1_attn_rel_emb",
}
for _t in "qkv":
    for _p in ("node_w", "node_b", "w1", "b1", "w2", "b2"):
        L1_WEIGHT_NAMES[f"l1_{_t}_{_p}"] = f"L1_ffn_{_t}_{_p}"

L2_WEIGHT_NAMES = {
    "l2_win_f32": "L2_attn_W_in", "l2_dt_bias": "L2_attn_dt_bias",
    "l2_B_bias": "L2_attn_B_bias", "l2_C_bias": "L2_attn_C_bias",
    "l2_Bn_w": "L2_attn_Bn_w", "l2_Cn_w": "L2_attn_Cn_w", "l2_D": "L2_attn_D",
    "l2_wout_f32": "L2_attn_W_out",
    "l2_gate_w": "L2_ffn_gate_w", "l2_sh_gate": "L2_ffn_sh_gate",
    "l2_sh_up": "L2_ffn_sh_up", "l2_sh_down": "L2_ffn_sh_down",
    "l2_rt_gate": "L2_ffn_rt_gate", "l2_rt_up": "L2_ffn_rt_up",
    "l2_rt_down": "L2_ffn_rt_down",
}

L3_WEIGHT_NAMES = {
    "l3_wq": "L3_attn_wq", "l3_wk": "L3_attn_wk", "l3_wv": "L3_attn_wv",
    "l3_wo": "L3_attn_wo", "l3_bo": "L3_attn_bo", "l3_proj": "L3_attn_proj",
    "l3_w_router": "L3_ffn_w_router", "l3_W_qkv": "L3_ffn_W_qkv",
    "l3_W_out": "L3_ffn_W_out", "l3_g1": "L3_ffn_g1", "l3_g2": "L3_ffn_g2",
    "l3_W1": "L3_ffn_W1", "l3_W2": "L3_ffn_W2", "l3_W3": "L3_ffn_W3",
}

L0_WEIGHT_NAMES = {
    "l0_qk_f32": "L0_attn_proj_qk_w", "l0_qkb": "L0_attn_proj_qk_b",
    "l0_v_f32": "L0_attn_proj_v_w", "l0_vb": "L0_attn_proj_v_b",
    "l0_inp_f32": "L0_attn_in_proj_w", "l0_inpb": "L0_attn_in_proj_b",
    "l0_out_f32": "L0_attn_out_proj_w", "l0_outb": "L0_attn_out_proj_b",
    "l0_Wg": "L0_ffn_Wg", "l0_A": "L0_ffn_A", "l0_B": "L0_ffn_B",
}


def _t5_bucket(rel):
    ret = 16 if rel > 0 else 0
    rp = abs(rel)
    if rp < 8:
        return ret + rp
    rp_safe = max(rp, 1)
    v = np.float32(np.log(np.float32(rp_safe) / np.float32(8.0)))
    v = v / np.float32(np.log(np.float32(16.0))) * np.float32(8)
    b = 8 + int(v)
    return ret + min(b, 15)


def build_consts_np():
    C = build_constants()
    pat = np.zeros((8, 128), np.float32)
    pat[np.arange(128) // 16, np.arange(128)] = 1.0
    C["pat8"] = pat.astype(BF)
    pat2 = np.zeros((2, 128), np.float32)
    pat2[np.arange(128) // 64, np.arange(128)] = 1.0
    C["pat2"] = pat2.astype(BF)
    pat16 = np.zeros((16, 8 * 128), np.float32)
    for p in range(8):
        x = np.arange(128)
        pat16[2 * p + x // 64, p * 128 + x] = 1.0
    C["pat16"] = pat16.astype(BF)
    tri = np.where(np.arange(128)[:, None] > np.arange(128)[None, :],
                   -30000.0, 0.0).astype(np.float32)
    C["trineg4"] = np.tile(tri, (1, 4)).astype(BF)
    r32f = np.zeros((64, 2048), np.float32)
    for h in range(64):
        r32f[h, h * 32:(h + 1) * 32] = 1.0
    C["rep32full"] = r32f.astype(BF)
    r32s = np.zeros((64, 16 * 128), np.float32)
    for g in range(16):
        p = np.arange(128)
        r32s[g * 4 + p // 32, g * 128 + p] = 1.0
    C["rep32sel"] = r32s.astype(BF)
    ohr = np.zeros((32, 8192), np.float32)
    for i in range(8192):
        rel = 4095 - i
        ohr[_t5_bucket(rel), i] = 1.0
    C["onehotrev"] = ohr
    r = np.arange(128)
    C["trilo"] = (r[None, :] < r[:, None]).astype(np.float32)
    C["triuneg"] = np.where(r[None, :] >= r[:, None], 0.0, -30000.0).astype(BF)
    C["iota128"] = r[:, None].astype(np.float32)
    return C


def declare_weights(nc, names, inputs):
    W = {}
    for loc, nm in names.items():
        arr = inputs[nm]
        W[loc] = nc.dram_tensor(nm, list(arr.shape), F32, kind="ExternalInput").ap()
    return W


def declare_consts(nc, C):
    aps = {}
    for k, v in C.items():
        dt = BF16 if v.dtype == BF else F32
        aps[k] = nc.dram_tensor("const_" + k, list(v.shape), dt,
                                kind="ExternalInput").ap()
    return aps


def build_layer_program(layer, inputs, taps=()):
    nc = bass.Bass("TRN2", target_bir_lowering=False)
    es = contextlib.ExitStack()
    S = Sched(nc)
    S.alloc_sems(es)
    cx = Ctx(nc, S, es)
    cx.taps = taps
    nsrc = {0: 24, 1: 32, 2: 24, 3: 100}.get(layer, 32)
    nw = 72 if layer == 3 else 130
    cx.wpool = WPool(cx, "wpool", nw, 128)
    cx.srcpool = WPool(cx, "srcpool", nsrc, 512)
    C = build_consts_np()
    cx.consts = declare_consts(nc, C)
    x_in = nc.dram_tensor("x_in", [S_LEN, D], F32, kind="ExternalInput").ap()
    x_out = nc.dram_tensor("x_out", [S_LEN, D], F32, kind="ExternalOutput").ap()
    load_consts(cx, cx.consts)
    S.op('dve', lambda E: E.memset(cx.sbuf("ones_col_f", (128, 1), F32)[:], 1.0),
         writes=("c_ones_col",))
    S.op('dve', lambda E: E.memset(cx.sbuf("ones_col_b", (128, 1), BF16)[:], 1.0),
         writes=("c_ones_col_b",))
    xn = cx.dram_scratch("xn_tm", (S_LEN, D), BF16)
    xnlo = cx.dram_scratch("xnlo_tm", (S_LEN, D), BF16)
    battn = cx.dram_scratch("battn", (D, S_LEN), F32)
    bffn = cx.dram_scratch("bffn", (D, S_LEN), F32)
    emit_layernorm(cx, x_in, xn, xnlo, layer, keep_stats=(layer == 3))
    if layer == 0:
        W = declare_weights(nc, L0_WEIGHT_NAMES, inputs)
        emit_L0_attn(cx, W, xn, battn)
        emit_L0_ffn(cx, W, xn, xnlo, bffn)
    elif layer == 1:
        W = declare_weights(nc, L1_WEIGHT_NAMES, inputs)
        emit_L1_attn(cx, W, xn, battn)
        emit_L1_ffn(cx, W, xn, xnlo, bffn)
    elif layer == 2:
        W = declare_weights(nc, L2_WEIGHT_NAMES, inputs)
        emit_L2_attn(cx, W, xn, battn)
        emit_L2_ffn(cx, W, xn, xnlo, bffn)
    elif layer == 3:
        W = declare_weights(nc, L3_WEIGHT_NAMES, inputs)
        stab_in = nc.dram_tensor("stab_all", [1, 8], F32,
                                 kind="ExternalInput").ap()
        bffn3 = cx.dram_scratch("bffn3_tm", (S_LEN, D), F32)
        emit_L3_attn(cx, W, xn, battn, stab_in)
        emit_L3_ffn(cx, W, xn, xnlo, bffn3)
        emit_combine2(cx, x_in, [battn], x_out, extra_tm=bffn3)
        S.finish()
        S.build_block()
        return nc, es, C
    else:
        raise NotImplementedError(layer)
    emit_combine2(cx, x_in, [battn, bffn], x_out)
    S.finish()
    S.build_block()
    return nc, es, C


def build_stab_program(inputs):
    nc = bass.Bass("TRN2", target_bir_lowering=False)
    es = contextlib.ExitStack()
    S = Sched(nc)
    S.alloc_sems(es)
    cx = Ctx(nc, S, es)
    cx.taps = ()
    cx.wpool = WPool(cx, "wpool", 80, 128)
    cx.srcpool = WPool(cx, "srcpool", 24, 512)
    C = build_consts_np()
    cx.consts = declare_consts(nc, C)
    x_in = nc.dram_tensor("x_in", [S_LEN, D], F32, kind="ExternalInput").ap()
    stab_out = nc.dram_tensor("stab_out", [1, 1], F32, kind="ExternalOutput").ap()
    load_consts(cx, cx.consts)
    S.op('dve', lambda E: E.memset(cx.sbuf("ones_col_f", (128, 1), F32)[:], 1.0),
         writes=("c_ones_col",))
    S.op('dve', lambda E: E.memset(cx.sbuf("ones_col_b", (128, 1), BF16)[:], 1.0),
         writes=("c_ones_col_b",))
    xn = cx.dram_scratch("xn_tm", (S_LEN, D), BF16)
    xnlo = cx.dram_scratch("xnlo_tm", (S_LEN, D), BF16)
    emit_layernorm(cx, x_in, xn, xnlo, 3)
    W = declare_weights(nc, L3_WEIGHT_NAMES, inputs)
    kD = cx.dram_scratch("l3_kD", (D, S_LEN), BF16)
    emit_l3_kproj(cx, W, xn, kD, tagsfx="s")
    projT = emit_l3_projT(cx, W)
    sp = cx.srcpool
    sp.reset()
    kh_, kkh = sp.alloc(8)
    kh = kh_[0:64, :]
    mx = cx.sbuf("st_mx", (128, 512), F32)
    mc = cx.sbuf("st_mc", (128, 1), F32)
    mr = cx.sbuf("st_mr", (1, 128), F32)
    s1 = cx.sbuf("st_s1", (1, 1), F32)
    psdd = cx.psum("l3_psdd", (128, 256), F32, bank=0)
    pst1 = cx.psum("st_pst", (1, 128), F32, bank=2)
    for h in range(16):
        S.dma('sp', kh[:], kD[h * 64:(h + 1) * 64, :], writes=kkh)
        for t in range(NTCH):
            t0 = t * 128
            S.op('pe', lambda E, t0=t0: E.matmul(psdd[:], kh[:, t0:t0 + 128],
                                                 projT[:], start=True, stop=True),
                 reads=kkh + ("l3_projT",), writes=("PSB0",))
            S.op('dve', lambda E, h=h, t=t: E.reduce_max(
                    mx[:, h * 32 + t:h * 32 + t + 1], psdd[:], axis=AX.X),
                 reads=("PSB0",), writes=("st_mx",))
    S.op('dve', lambda E: E.reduce_max(mc[:], mx[:], axis=AX.X),
         reads=("st_mx",), writes=("st_mc",))
    S.op('pe', lambda E: E.transpose(pst1[:], mc[:], cx.sb["ident_f"][:]),
         reads=("st_mc", "c_ident_f"), writes=("st_pst",))
    S.op('act', lambda E: E.activation(mr[:], pst1[:], AF.Copy),
         reads=("st_pst",), writes=("st_mr",))
    S.op('dve', lambda E: E.reduce_max(s1[:], mr[:], axis=AX.X),
         reads=("st_mr",), writes=("st_s1",))
    S.op('act', lambda E: E.activation(s1[:], s1[:], AF.Copy, scale=L3C),
         reads=("st_s1",), writes=("st_s1",))
    S.dma('sp', stab_out[:, :], s1[:], reads=("st_s1",))
    S.finish()
    S.build_block()
    return nc, es, C


def run_layer(layer, inputs, x_all, taps=(), ret_raw=False):
    names = {0: L0_WEIGHT_NAMES, 1: L1_WEIGHT_NAMES, 2: L2_WEIGHT_NAMES,
             3: L3_WEIGHT_NAMES}[layer]
    stabs = None
    if layer == 3:
        nc, es, C = build_stab_program(inputs)
        in_maps = []
        for b in range(8):
            m = {"x_in": np.ascontiguousarray(x_all[b])}
            for loc, nm in names.items():
                m[nm] = np.ascontiguousarray(np.asarray(inputs[nm], np.float32))
            for k, v in C.items():
                m["const_" + k] = np.ascontiguousarray(v)
            in_maps.append(m)
        res = run_bass_kernel_spmd(nc, in_maps, core_ids=list(range(8)))
        es.close()
        stabs = np.concatenate(
            [np.asarray(res.results[b]["stab_out"]).reshape(1, 1)
             for b in range(8)], axis=1).astype(np.float32)
    nc, es, C = build_layer_program(layer, inputs, taps)
    in_maps = []
    for b in range(8):
        m = {"x_in": np.ascontiguousarray(x_all[b])}
        for loc, nm in names.items():
            m[nm] = np.ascontiguousarray(np.asarray(inputs[nm], np.float32))
        for k, v in C.items():
            m["const_" + k] = np.ascontiguousarray(v)
        if stabs is not None:
            m["stab_all"] = np.ascontiguousarray(stabs)
        in_maps.append(m)
    res = run_bass_kernel_spmd(nc, in_maps, core_ids=list(range(8)))
    es.close()
    if ret_raw:
        return res
    return np.stack([res.results[b]["x_out"] for b in range(8)], 0)


N_LAYERS_ENABLED = 4


def kernel(**inputs):
    x = np.asarray(inputs["L0_attn_x"], np.float32)
    for layer in range(N_LAYERS_ENABLED):
        x = run_layer(layer, inputs, x)
    return x



def emit_dense_tm(cx, tag, wget_rhs, src_lhsT, n_ic, n_oh, out_dram, ow=512,
                  bias=None, extra=None):
    S = cx.S
    ps = [cx.psum(f"dtps{i}", (128, 512)) for i in range(2)]
    bufs = [cx.sbuf(f"{tag}_to{i}", (128, ow), BF16) for i in range(2)]
    for tc in range(NTCH):
        for oh in range(n_oh):
            p = ps[oh % 2]
            pskey = f"dtps{oh % 2}"
            mms = []
            for ic in range(n_ic):
                s_ap, sk = src_lhsT(ic, tc)
                w_ap, wk = wget_rhs(ic, oh)
                mms.append((s_ap, sk, w_ap, wk))

            def fn(E, mms=mms, p=p, tc=tc, oh=oh):
                inst = None
                for i, (s_ap, sk, w_ap, wk) in enumerate(mms):
                    inst = E.matmul(p[:, 0:ow], s_ap, w_ap, start=(i == 0),
                                    stop=(i == len(mms) - 1 and extra is None))
                if extra is not None:
                    inst = extra(E, p, tc, oh)
                return inst
            reads = tuple(set(k for m in mms for k in (m[1], m[3])))
            S.op('pe', fn, reads=reads, writes=(pskey,))
            b = bufs[oh % 2]
            bk = f"{tag}_to{oh % 2}"
            kw = {}
            rd = [pskey]
            fn_ = AF.Copy
            if bias is not None:
                bt, btk = bias
                kw['bias'] = bt[:, oh:oh + 1]
                rd.append(btk)
                fn_ = AF.Identity
            S.op('act', lambda E, p=p, b=b, kw=kw, fn_=fn_: E.activation(
                    b[:, 0:ow], p[:, 0:ow], fn_, **kw),
                 reads=tuple(rd), writes=(bk,))
            S.dma('sp', out_dram[tc * 128:(tc + 1) * 128, oh * ow:(oh + 1) * ow],
                  b[:, 0:ow], reads=(bk,))


class FmSrc128:
    def __init__(self, cx, name, tm_dram, pool, n_ic=NDCH):
        self.cx, self.name, self.tm = cx, name, tm_dram
        self.tiles = [pool.alloc(1) for _ in range(n_ic)]
        self.cur = [None] * n_ic

    def get(self, ic, tc):
        ap, keys = self.tiles[ic]
        ap = ap[:, 0:128]
        if self.cur[ic] != tc:
            self.cx.S.dma('sp', ap,
                          self.tm[tc * 128:(tc + 1) * 128, ic * 128:(ic + 1) * 128],
                          writes=keys, transpose=True)
            self.cur[ic] = tc
        return ap, keys[0]



def emit_L1_attn(cx, W, xn_tm, battn_dram):
    S = cx.S
    H, DK = 16, 64
    Qf = cx.dram_scratch("l1_Q", (D, S_LEN), BF16)
    Kf = cx.dram_scratch("l1_K", (D, S_LEN), BF16)
    Vt = cx.dram_scratch("l1_Vt", (S_LEN, D), BF16)
    AOf = cx.dram_scratch("l1_AO", (D, S_LEN), BF16)
    AddRev = cx.dram_scratch("l1_addrev", (16, 8192), F32)

    sp, wp = cx.srcpool, cx.wpool
    sp.reset(); wp.reset()
    xnsrc = FmSrc(cx, "l1xn", xn_tm, sp)
    wq = WJax(cx, "l1wq", W["l1_wq"], 1024, 1024, wp)
    emit_dense_fm(cx, "l1q", wq.get, xnsrc.get, NDCH, NDCH,
                  sink_to_dram(cx, "l1q", Qf))
    sp.reset(); wp.reset()
    xnsrc = FmSrc(cx, "l1xnb", xn_tm, sp)
    wk = WJax(cx, "l1wk", W["l1_wk"], 1024, 1024, wp)
    emit_dense_fm(cx, "l1k", wk.get, xnsrc.get, NDCH, NDCH,
                  sink_to_dram(cx, "l1k", Kf))
    sp.reset(); wp.reset()
    x128 = FmSrc128(cx, "l1x8", xn_tm, sp)
    wv = WJax(cx, "l1wv", W["l1_wv"], 1024, 1024, wp)
    emit_dense_tm(cx, "l1v", lambda ic, oh: (wv.get(ic, oh, ow=512)[0],
                                             wv.get(ic, oh, ow=512)[1]),
                  x128.get, NDCH, 2, Vt)

    remb = cx.sbuf("l1_remb", (32, 16), F32)
    S.dma('sp', remb[:], W["l1_rel_emb"], writes=("l1_remb",))
    rembT = cx.sbuf("l1_rembT", (16, 32), F32)
    rembRow = cx.sbuf("l1_rembRow", (1, 512), F32)
    ptT = cx.psum("l1_ptT", (16, 32), F32, bank=2)
    S.op('pe', lambda E: E.transpose(ptT[:], remb[:], cx.sb["ident_f"][0:32, 0:32]),
         reads=("l1_remb", "c_ident_f"), writes=("l1_ptT",))
    S.op('dve', lambda E: E.tensor_copy(rembT[:], ptT[:]),
         reads=("l1_ptT",), writes=("l1_rembT",))
    S.dma('sp', rembRow[:], W["l1_rel_emb"].rearrange("b h -> (b h)").unsqueeze(0),
          writes=("l1_rembRow",))
    oh_rev = cx.sbuf("l1_ohrev", (32, 512), F32)
    av_ps = cx.psum("l1_avps", (16, 512), F32, bank=3)
    av_sb = cx.sbuf("l1_avsb", (16, 512), F32)
    for sl in range(16):
        S.dma('sp', oh_rev[:], cx.consts["onehotrev"][:, sl * 512:(sl + 1) * 512],
              writes=("l1_ohrev",))
        S.op('pe', lambda E: E.matmul(av_ps[:], remb[:, 0:16], oh_rev[:],
                                      start=True, stop=True),
             reads=("l1_remb", "l1_ohrev"), writes=("l1_avps",))
        S.op('dve', lambda E: E.tensor_scalar(av_sb[:], av_ps[:], rembT[:, 15:16],
                                              None, ALU.subtract),
             reads=("l1_avps", "l1_rembT"), writes=("l1_avsb",))
        S.dma('sp', AddRev[:, sl * 512:(sl + 1) * 512], av_sb[:], reads=("l1_avsb",))

    kbuf = cx.sbuf("l1a_k", (128, S_LEN), BF16)
    vbuf = cx.sbuf("l1a_v", (128, S_LEN), BF16)
    qtl = cx.sbuf("l1a_q", (128, 512), BF16)
    band = cx.sbuf("l1a_band", (128, 6 * 512 * 2), BF16)
    pt = [cx.sbuf(f"l1a_p{i}", (128, 512), BF16) for i in range(2)]
    lrow = [cx.sbuf(f"l1a_l{i}", (1, 512), F32) for i in range(2)]
    rb = cx.sbuf("l1a_rb", (64, 512), F32)
    ob = cx.sbuf("l1a_ob", (64, 512), BF16)
    idxt = cx.sbuf("l1a_idx", (128, 12), U32)
    psS = [cx.psum(f"l1a_psS{i}", (128, 512), F32, bank=i) for i in range(2)]
    psL = [cx.psum(f"l1a_psL{i}", (1, 512), F32, bank=2 + i) for i in range(2)]
    psA = [cx.psum(f"l1a_psA{i}", (64, 512), F32, bank=4 + i) for i in range(2)]
    psR = cx.psum("l1a_psR", (64, 512), F32, bank=6)
    onesb = cx.sbuf("ones_col_b", (128, 1), BF16)
    identb = cx.sb["ident_b"]
    DIFFS = [-128, 0, 128, 256, 384, 512]
    for hp in range(8):
        S.dma('sp', kbuf[:], Kf[hp * 128:(hp + 1) * 128, :], writes=("l1a_kb",))
        for tc in range(NTCH):
            S.dma('sp', vbuf[:, tc * 128:(tc + 1) * 128],
                  Vt[tc * 128:(tc + 1) * 128, hp * 128:(hp + 1) * 128],
                  writes=(f"l1a_vb{tc}",))
        for hh in range(2):
            h = 2 * hp + hh
            mq, kq = emit_maxnorm2(cx, f"l1q{h}", Qf, h * 64, 64)
            mk, kk2 = emit_maxnorm2(cx, f"l1k{h}", Kf, h * 64, 64)
            c, ck = emit_scalar_sqrtmul(cx, mq, kq, mk, kk2, f"l1c{h}", slack=1.0)
            for cs, col in (("n", 240), ("p", 496)):
                t = cx.sbuf(cx.uname("l1bc"), (1, 1), F32)
                S.op('dve', lambda E, t=t, col=col, c=c, h=h: E.tensor_sub(
                        t[:], rembRow[0:1, col + h:col + h + 1], c[:]),
                     reads=("l1_rembRow", ck), writes=(f"l1bs{cs}{h}",))
                bcast_col(cx, t[:], f"l1bs{cs}{h}", f"l1bc{cs}{h}",
                          name=f"bcol_{cs}{h}")
            bandf = cx.sbuf("l1a_bandf", (128, 512), F32)
            for dn, diff in enumerate(DIFFS):
                off = h * 8192 + 4095 - diff - 127
                src = BassAP(tensor=AddRev.tensor, offset=off,
                             ap=[[1, 128], [1, 512]])
                S.dma('pool', bandf[:], src, writes=("l1a_bandf",))
                S.op('dve', lambda E, hh=hh, dn=dn: E.tensor_copy(
                        band[:, (hh * 6 + dn) * 512:(hh * 6 + dn + 1) * 512], bandf[:]),
                     reads=("l1a_bandf",), writes=(f"l1a_band{hh}_{dn}",))
        for qt in range(8):
            for hh in range(2):
                h = 2 * hp + hh
                S.dma('sp', qtl[:], Qf[hp * 128:(hp + 1) * 128, qt * 512:(qt + 1) * 512],
                      writes=("l1a_q",)) if hh == 0 else None
                psA_h = psA[hh]
                first_av = [True]
                for kc in range(NTCH):
                    diff = kc * 128 - qt * 512
                    bandidx = DIFFS.index(diff) if diff in DIFFS else None
                    pb = psS[hh]
                    pskey = f"PSB{hh}"

                    def fnS(E, kc=kc, hh=hh, bandidx=bandidx, pb=pb):
                        inst = E.matmul(pb[:], kbuf[hh * 64:(hh + 1) * 64,
                                                    kc * 128:(kc + 1) * 128],
                                        qtl[hh * 64:(hh + 1) * 64, :],
                                        start=True, stop=(bandidx is None))
                        if bandidx is not None:
                            bi = (hh * 6 + bandidx) * 512
                            inst = E.matmul(pb[:], cx.sb["rev_b"][:],
                                            band[:, bi:bi + 512],
                                            start=False, stop=True)
                        return inst
                    rd = ["l1a_kb", "l1a_q", "c_rev_b"]
                    if bandidx is not None:
                        rd.append(f"l1a_band{hh}_{bandidx}")
                    S.op('pe', fnS, reads=tuple(rd), writes=(pskey,))
                    case = "p" if diff >= 640 else "n"
                    bcol = cx.sb[f"bcol_{case}{h}"]
                    S.op('act', lambda E, pb=pb, hh=hh, bcol=bcol: E.activation(
                            pt[hh][:], pb[:], AF.Exp, bias=bcol[:]),
                         reads=(pskey, f"l1bc{case}{h}"), writes=(f"l1a_p{hh}",))
                    S.op('pe', lambda E, hh=hh, kc=kc: E.matmul(
                            psL[hh][:], onesb[:], pt[hh][:],
                            start=(kc == 0), stop=(kc == NTCH - 1)),
                         reads=(f"l1a_p{hh}", "c_ones_col_b"), writes=(f"PSB{2 + hh}",))
                    S.op('pe', lambda E, hh=hh, kc=kc, psA_h=psA_h: E.matmul(
                            psA_h[:],
                            vbuf[:, kc * 128 + hh * 64:kc * 128 + (hh + 1) * 64],
                            pt[hh][:],
                            start=(kc == 0), stop=(kc == NTCH - 1)),
                         reads=(f"l1a_p{hh}", f"l1a_vb{kc}"), writes=(f"PSB{4 + hh}",))
                S.op('dve', lambda E, hh=hh: E.reciprocal(lrow[hh][:], psL[hh][:]),
                     reads=(f"PSB{2 + hh}",), writes=(f"l1a_lr{hh}",))
                S.op('pe', lambda E, hh=hh: E.matmul(psR[:],
                                                     cx.sb["ones_row_f"][:, 0:64],
                                                     lrow[hh][:], start=True, stop=True),
                     reads=(f"l1a_lr{hh}", "c_ones_f"), writes=("PSB6",))
                S.op('act', lambda E: E.activation(rb[:], psR[:], AF.Copy),
                     reads=("PSB6",), writes=("l1a_rb",))
                S.op('dve', lambda E, hh=hh, psA_h=psA_h: E.tensor_mul(
                        ob[:], psA_h[:], rb[:]),
                     reads=(f"PSB{4 + hh}", "l1a_rb"), writes=("l1a_ob",))
                S.dma('sp', AOf[h * 64:(h + 1) * 64, qt * 512:(qt + 1) * 512], ob[:],
                      reads=("l1a_ob",))
    sp.reset(); wp.reset()
    aosrc = FmDirect(cx, "l1ao", AOf, sp)
    wo = WJax(cx, "l1wo", W["l1_wo"], 1024, 1024, wp)
    emit_dense_fm(cx, "l1o", wo.get, aosrc.get, NDCH, NDCH,
                  sink_to_dram(cx, "l1o", battn_dram, dtype=F32))



def emit_L1_ffn(cx, W, xn_tm, xnlo_tm, bffn_dram):
    S = cx.S
    sp, wp = cx.srcpool, cx.wpool
    nwt = cx.sbuf("t1_nw", (15, 128), F32)
    ptn = cx.psum("t1_ptn", (128, 15), F32, bank=2)
    nts = cx.sbuf("t1_nts", (128, 15), F32)
    for tree in "qkv":
        nodeT = cx.dram_scratch(f"t1_nodeT_{tree}", (1024, 15), F32)
        for ic in range(8):
            S.dma('sp', nwt[:], W[f"l1_{tree}_node_w"][:, ic * 128:(ic + 1) * 128],
                  writes=("t1_nw",))
            S.op('pe', lambda E: E.transpose(ptn[:], nwt[:], cx.sb["ident_f"][0:15, 0:15]),
                 reads=("t1_nw", "c_ident_f"), writes=("t1_ptn",))
            S.op('dve', lambda E: E.tensor_copy(nts[:], ptn[:]),
                 reads=("t1_ptn",), writes=("t1_nts",))
            S.dma('sp', nodeT[ic * 128:(ic + 1) * 128, :], nts[:], reads=("t1_nts",))
    Qf = cx.dram_scratch("t1_Q", (D, S_LEN), BF16)
    Kf = cx.dram_scratch("t1_K", (D, S_LEN), BF16)
    Vt = cx.dram_scratch("t1_Vt", (S_LEN, D), BF16)
    outs = {"q": ("fm", Qf), "k": ("fm", Kf), "v": ("tm", Vt)}
    selbuf = cx.sbuf("t1_sel", (16, S_LEN), BF16)
    dtile = cx.sbuf("t1_d", (15, 512), BF16)
    nb = cx.sbuf("t1_nb", (15, 1), F32)
    lgt = cx.sbuf("t1_lgt", (15, 512), F32)
    msum = cx.sbuf("t1_ms", (16, 512), F32)
    psm = cx.psum("t1_psm", (16, 512), F32, bank=2)
    treeM = cx.sb["treeM"]
    treeC = cx.sb["treeC"]
    hwbufs = [cx.sbuf(f"t1_hw{p}", (128, 512), BF16) for p in range(8)]
    t1ob = cx.sbuf("t1_ob", (128, 512), BF16)
    selb = cx.sbuf("t1_selb", (128, 512), BF16)
    psH = cx.psum("t1_psH", (128, 512), F32, bank=3)
    psSb = cx.psum("t1_psSb", (128, 512), F32, bank=4)
    psO = [cx.psum(f"t1_psO{i}", (128, 512), F32, bank=i) for i in range(2)]
    for tree in "qkv":
        kind, outd = outs[tree]
        sp.reset(); wp.reset()
        xnh = FmSrc(cx, f"t1xh{tree}", xn_tm, sp)
        xnl = FmSrc(cx, f"t1xl{tree}", xnlo_tm, sp)
        lg, lgkey = emit_sel_logits(cx, f"t1s{tree}", xnh.get, xnl.get,
                                    cx.dram[f"t1_nodeT_{tree}"], 15)
        S.dma('sp', nb[:], W[f"l1_{tree}_node_b"].unsqueeze(1), writes=("t1_nb",))
        S.op('dve', lambda E: E.tensor_single_scalar(nb[:], nb[:], -1.0, ALU.mult),
             reads=("t1_nb",), writes=("t1_nb",))
        for tt in range(8):
            S.dma('sp', lgt[:], lg[:, tt * 512:(tt + 1) * 512], writes=("t1_lgt",))
            S.op('dve', lambda E: E.tensor_scalar(dtile[:], lgt[:], nb[:], None,
                                                  ALU.is_ge),
                 reads=("t1_lgt", "t1_nb"), writes=("t1_d",))
            S.op('pe', lambda E: E.matmul(psm[:], treeM[:], dtile[:],
                                          start=True, stop=True),
                 reads=("t1_d", "c_treeM"), writes=("PSB2",))
            S.op('act', lambda E: E.activation(msum[:], psm[:], AF.Identity,
                                               bias=treeC[:]),
                 reads=("PSB2", "c_treeC"), writes=("t1_ms",))
            S.op('dve', lambda E, tt=tt: E.tensor_single_scalar(
                    selbuf[:, tt * 512:(tt + 1) * 512], msum[:], 3.5, ALU.is_ge),
                 reads=("t1_ms",), writes=(f"t1_sel{tt}",))
        sp.reset(); wp.reset()
        w1cat = cx.dram_scratch(f"t1_w1cat_{tree}", (1024, 1024), BF16)
        for l in range(16):
            emit_cast_bf16(cx, W[f"l1_{tree}_w1"][l], w1cat[:, l * 64:(l + 1) * 64],
                           1024, 64, f"w1{tree}{l}")
        w1 = WJaxBF(cx, f"t1w1{tree}", w1cat, 1024, 1024, wp)
        b1c = cx.sbuf(f"t1_b1{tree}", (128, 8), F32)
        b1r = cx.sbuf("t1_b1r", (8, 128), F32)
        psb1 = cx.psum("t1_psb1", (128, 8), F32, bank=5)
        S.dma('sp', b1r[:], W[f"l1_{tree}_b1"].rearrange("(l2 l) h -> l2 (l h)", l2=8),
              writes=("t1_b1r",))
        S.op('pe', lambda E: E.transpose(psb1[:], b1r[:], cx.sb["ident_f"][0:8, 0:8]),
             reads=("t1_b1r", "c_ident_f"), writes=("PSB5",))
        S.op('act', lambda E, b1c=b1c: E.activation(b1c[:], psb1[:], AF.Copy),
             reads=("PSB5",), writes=(f"t1_b1{tree}",))
        w2 = WJax(cx, f"t1w2{tree}",
                  W[f"l1_{tree}_w2"].rearrange("l h d -> (l h) d"), 1024, 1024, wp)
        b2sb = cx.sbuf("t1_b2sb", (16, 1024), BF16)
        S.dma('pool', b2sb[:], W[f"l1_{tree}_b2"], writes=("t1_b2sb",))
        xns = FmSrc(cx, f"t1x2{tree}", xn_tm, sp)
        if kind == "fm":
            for tt in range(8):
                for p in range(8):
                    mms = [(*w1.get(ic, p), *xns.get(ic, tt)) for ic in range(NDCH)]

                    def fnh(E, mms=mms):
                        inst = None
                        for i, (w_ap, wk, x_ap, xk) in enumerate(mms):
                            inst = E.matmul(psH[:], w_ap, x_ap, start=(i == 0),
                                            stop=(i == len(mms) - 1))
                        return inst
                    S.op('pe', fnh, reads=tuple(set(k for m in mms for k in (m[1], m[3]))),
                         writes=("PSB3",))
                    S.op('pe', lambda E, p=p, tt=tt: E.matmul(
                            psSb[:], cx.sb["pat16"][:, p * 128:(p + 1) * 128],
                            selbuf[:, tt * 512:(tt + 1) * 512],
                            start=True, stop=True),
                         reads=(f"t1_sel{tt}", "c_pat16"), writes=("PSB4",))
                    S.op('act', lambda E: E.activation(selb[:], psSb[:], AF.Copy),
                         reads=("PSB4",), writes=("t1_selb",))
                    S.op('act', lambda E, p=p, b1c=b1c: E.activation(
                            hwbufs[p][:], psH[:], AF.Relu, bias=b1c[:, p:p + 1]),
                         reads=("PSB3", f"t1_b1{tree}"), writes=(f"t1_hw{p}a",))
                    S.op('dve', lambda E, p=p: E.tensor_mul(hwbufs[p][:], hwbufs[p][:],
                                                            selb[:]),
                         reads=(f"t1_hw{p}a", "t1_selb"), writes=(f"t1_hw{p}",))
                for oc in range(NDCH):
                    po = psO[oc % 2]
                    pok = f"PSB{oc % 2}"
                    wtiles = [w2.get(p, oc) for p in range(8)]

                    def fno(E, oc=oc, tt=tt, po=po, wtiles=wtiles):
                        inst = None
                        for p in range(8):
                            inst = E.matmul(po[:], wtiles[p][0], hwbufs[p][:],
                                            start=(p == 0), stop=False)
                        inst = E.matmul(po[:], b2sb[:, oc * 128:(oc + 1) * 128],
                                        selbuf[:, tt * 512:(tt + 1) * 512],
                                        start=False, stop=True)
                        return inst
                    rds = [w[1] for w in wtiles] + ["t1_b2sb"] + \
                          [f"t1_hw{p}" for p in range(8)] + [f"t1_sel{tt}"]
                    S.op('pe', fno, reads=tuple(set(rds)), writes=(pok,))
                    S.op('act', lambda E, po=po: E.activation(t1ob[:], po[:], AF.Copy),
                         reads=(pok,), writes=("t1_ob",))
                    S.dma('sp', outd[oc * 128:(oc + 1) * 128, tt * 512:(tt + 1) * 512],
                          t1ob[:], reads=("t1_ob",))
        else:
            x128 = FmSrc128(cx, f"t1x8{tree}", xn_tm, sp)
            for tc in range(NTCH):
                tt = tc // 4
                for p in range(8):
                    mms = [(*w1.get(ic, p), *x128.get(ic, tc)) for ic in range(NDCH)]

                    def fnh(E, mms=mms):
                        inst = None
                        for i, (w_ap, wk, x_ap, xk) in enumerate(mms):
                            inst = E.matmul(psH[:, 0:128], w_ap, x_ap, start=(i == 0),
                                            stop=(i == len(mms) - 1))
                        return inst
                    S.op('pe', fnh, reads=tuple(set(k for m in mms for k in (m[1], m[3]))),
                         writes=("PSB3",))
                    S.op('pe', lambda E, p=p, tc=tc: E.matmul(
                            psSb[:, 0:128], cx.sb["pat16"][:, p * 128:(p + 1) * 128],
                            selbuf[:, tc * 128:(tc + 1) * 128],
                            start=True, stop=True),
                         reads=(f"t1_sel{tc // 4}", "c_pat16"), writes=("PSB4",))
                    S.op('act', lambda E: E.activation(selb[:, 0:128], psSb[:, 0:128],
                                                       AF.Copy),
                         reads=("PSB4",), writes=("t1_selb",))
                    S.op('act', lambda E, p=p, b1c=b1c: E.activation(
                            hwbufs[p][:, 0:128], psH[:, 0:128], AF.Relu,
                            bias=b1c[:, p:p + 1]),
                         reads=("PSB3", f"t1_b1{tree}"), writes=(f"t1_hw{p}a",))
                    S.op('dve', lambda E, p=p: E.tensor_mul(
                            hwbufs[p][:, 0:128], hwbufs[p][:, 0:128], selb[:, 0:128]),
                         reads=(f"t1_hw{p}a", "t1_selb"), writes=(f"t1_hw{p}",))
                for oh in range(2):
                    po = psO[oh % 2]
                    pok = f"PSB{oh % 2}"

                    wtiles = [w2.get(p, oh, ow=512) for p in range(8)]

                    def fno(E, oh=oh, tc=tc, po=po, wtiles=wtiles):
                        inst = None
                        for p in range(8):
                            inst = E.matmul(po[:], hwbufs[p][:, 0:128], wtiles[p][0],
                                            start=(p == 0), stop=False)
                        inst = E.matmul(po[:], selbuf[:, tc * 128:(tc + 1) * 128],
                                        b2sb[:, oh * 512:(oh + 1) * 512],
                                        start=False, stop=True)
                        return inst
                    rds = [w[1] for w in wtiles] + ["t1_b2sb"] + \
                          [f"t1_hw{p}" for p in range(8)] + [f"t1_sel{tc // 4}"]
                    S.op('pe', fno, reads=tuple(set(rds)), writes=(pok,))
                    S.op('act', lambda E, po=po: E.activation(t1ob[:], po[:], AF.Copy),
                         reads=(pok,), writes=("t1_ob",))
                    S.dma('sp', outd[tc * 128:(tc + 1) * 128, oh * 512:(oh + 1) * 512],
                          t1ob[:], reads=("t1_ob",))

    scale = 1.0 / math.sqrt(1024)
    mq, kq = emit_maxnorm2(cx, "t1aq", Qf, 0, 128, nchunks=8)
    mk, kk2 = emit_maxnorm2(cx, "t1ak", Kf, 0, 128, nchunks=8)
    c, ck = emit_scalar_sqrtmul(cx, mq, kq, mk, kk2, "t1c")
    bcast_col(cx, c[:], ck, "t1_bc", scale=-scale, name="bcol_t1")
    bcol = cx.sb["bcol_t1"]
    kbuf = cx.sbuf("t1a_kb", (128, 256), BF16)
    qbuf = cx.sbuf("t1a_qb", (128, 4096), BF16)
    sp.reset()
    pbuf, pkeys = sp.alloc(32)
    vtl = cx.sbuf("t1a_vt", (128, 128), BF16)
    lrow = cx.sbuf("t1a_lr", (1, 512), F32)
    rbb = cx.sbuf("t1a_rb", (128, 512), F32)
    obf = cx.sbuf("t1a_ob", (128, 512), F32)
    psS2 = [cx.psum(f"t1a_psS{i}", (128, 512), F32, bank=i) for i in range(2)]
    psL2 = cx.psum("t1a_psL", (1, 512), F32, bank=5)
    psA2 = [cx.psum(f"t1a_psA{i}", (128, 512), F32, bank=2 + i) for i in range(2)]
    psR2 = cx.psum("t1a_psR", (128, 512), F32, bank=6)
    onesb = cx.sbuf("ones_col_b", (128, 1), BF16)
    for qt in range(8):
        for ic in range(NDCH):
            S.dma('sp', qbuf[:, ic * 512:(ic + 1) * 512],
                  Qf[ic * 128:(ic + 1) * 128, qt * 512:(qt + 1) * 512],
                  writes=(f"t1a_q{ic}",))
        for kc in range(NTCH):
            pb = psS2[kc % 2]
            pbk = f"PSB{kc % 2}"
            for ic in range(NDCH):
                kb = kbuf[:, (ic % 2) * 128:(ic % 2) * 128 + 128]
                S.dma('sp', kb, Kf[ic * 128:(ic + 1) * 128, kc * 128:(kc + 1) * 128],
                      writes=(f"t1a_kb{ic % 2}",))
                S.op('pe', lambda E, ic=ic, pb=pb, kb=kb: E.matmul(
                        pb[:], kb, qbuf[:, ic * 512:(ic + 1) * 512],
                        start=(ic == 0), stop=(ic == NDCH - 1)),
                     reads=(f"t1a_kb{ic % 2}", f"t1a_q{ic}"), writes=(pbk,))
            S.op('act', lambda E, pb=pb, kc=kc: E.activation(
                    pbuf[:, kc * 512:(kc + 1) * 512], pb[:], AF.Exp, scale=scale,
                    bias=bcol[:]),
                 reads=(pbk, "t1_bc"), writes=(pkeys[kc],))
            S.op('pe', lambda E, kc=kc: E.matmul(
                    psL2[:], onesb[:], pbuf[:, kc * 512:(kc + 1) * 512],
                    start=(kc == 0), stop=(kc == NTCH - 1)),
                 reads=(pkeys[kc], "c_ones_col_b"), writes=("PSB5",))
        S.op('dve', lambda E: E.reciprocal(lrow[:], psL2[:]),
             reads=("PSB5",), writes=("t1a_lr",))
        S.op('pe', lambda E: E.matmul(psR2[:], cx.sb["ones_row_f"][:], lrow[:],
                                      start=True, stop=True),
             reads=("t1a_lr", "c_ones_f"), writes=("PSB6",))
        S.op('act', lambda E: E.activation(rbb[:], psR2[:], AF.Copy),
             reads=("PSB6",), writes=("t1a_rbb",))
        for dc in range(NDCH):
            pa = psA2[dc % 2]
            pak = f"PSB{2 + dc % 2}"
            for kc in range(NTCH):
                S.dma('sp', vtl[:], Vt[kc * 128:(kc + 1) * 128, dc * 128:(dc + 1) * 128],
                      writes=("t1a_vt",))
                S.op('pe', lambda E, kc=kc, pa=pa: E.matmul(
                        pa[:], vtl[:], pbuf[:, kc * 512:(kc + 1) * 512],
                        start=(kc == 0), stop=(kc == NTCH - 1)),
                     reads=("t1a_vt", pkeys[kc]), writes=(pak,))
            S.op('dve', lambda E, pa=pa: E.tensor_mul(obf[:], pa[:], rbb[:]),
                 reads=(pak, "t1a_rbb"), writes=("t1a_obf",))
            S.dma('sp', bffn_dram[dc * 128:(dc + 1) * 128, qt * 512:(qt + 1) * 512],
                  obf[:], reads=("t1a_obf",))



def emit_softplus(cx, tag, out, in_ap, rkey, wkey, shape):
    S = cx.S
    t1 = cx.sbuf(f"{tag}_sp1", shape, F32)
    t2 = cx.sbuf(f"{tag}_sp2", shape, F32)
    one = cx.constcol(1.0, shape[0])
    S.op('act', lambda E: E.activation(t1[:], in_ap, AF.Abs),
         reads=(rkey,), writes=(f"{tag}_sp1",))
    S.op('act', lambda E: E.activation(t1[:], t1[:], AF.Exp, scale=-1.0),
         reads=(f"{tag}_sp1",), writes=(f"{tag}_sp1",))
    S.op('act', lambda E: E.activation(t1[:], t1[:], AF.Ln, bias=one[:]),
         reads=(f"{tag}_sp1", f"ccol{(1.0, shape[0])}"), writes=(f"{tag}_sp1",))
    S.op('dve', lambda E: E.tensor_single_scalar(t2[:], in_ap, 0.0, ALU.max),
         reads=(rkey,), writes=(f"{tag}_sp2",))
    S.op('dve', lambda E: E.tensor_add(out, t1[:], t2[:]),
         reads=(f"{tag}_sp1", f"{tag}_sp2"), writes=(wkey,))


def emit_L2_attn(cx, W, xn_tm, battn_dram):
    S = cx.S
    sp, wp = cx.srcpool, cx.wpool
    NH, HP, NS = 64, 32, 96
    DI = 2048
    for nm, (r, c) in [("win", (4432, 1024)), ("wout", (1024, 2048))]:
        dst = cx.dram_scratch(f"l2w_{nm}", (r, c), BF16)
        emit_cast_bf16(cx, W[f"l2_{nm}_f32"], dst, r, c, f"l2{nm}")
    bcpad = cx.dram_scratch("l2w_bcpad", (256, 1024), BF16)
    zt = cx.sbuf("l2_zt", (128, 1024), BF16)
    S.op('dve', lambda E: E.memset(zt[:], 0.0), writes=("l2_zt",))
    for i in range(2):
        S.dma('sp', bcpad[i * 128:(i + 1) * 128, :], zt[:], reads=("l2_zt",))
    for half, base in ((0, 4096), (1, 4160)):
        for dstrow, srow, n in ((0, 0, 16), (32, 16, 16), (64, 32, 32)):
            emit_cast_bf16(cx, W["l2_win_f32"][base + srow:base + srow + n, :],
                           bcpad[half * 128 + dstrow:half * 128 + dstrow + n, :],
                           n, 1024, f"bcp{half}{dstrow}")
    Zf = cx.dram_scratch("l2_Z", (S_LEN, DI), BF16)
    Xt = cx.dram_scratch("l2_Xt", (S_LEN, DI), BF16)
    Bp = cx.dram_scratch("l2_Bp", (128, S_LEN), F32)
    Cp = cx.dram_scratch("l2_Cp", (128, S_LEN), F32)
    AD = cx.dram_scratch("l2_AD", (128, S_LEN), F32)
    TG = cx.dram_scratch("l2_TG", (80, S_LEN), F32)
    Yf = cx.dram_scratch("l2_Y", (S_LEN, DI), BF16)
    Gt = cx.dram_scratch("l2_G", (S_LEN, DI), BF16)
    csD = cx.dram_scratch("l2_cs", (64, S_LEN), F32)

    for srcrows, dstt in ((0, Zf), (2048, Xt)):
        sp.reset(); wp.reset()
        x128 = FmSrc128(cx, f"l2x8{srcrows}", xn_tm, sp)
        wx = WTorch(cx, f"l2wx{srcrows}",
                    cx.dram["l2w_win"][srcrows:srcrows + 2048, :], 1024, 2048, wp)
        obuf = cx.sbuf("l2x_ob", (128, 512), BF16)
        psx = [cx.psum(f"dtps{i}", (128, 512)) for i in range(2)]
        emit_l2_tmproj(cx, x128, wx, psx, obuf, dstt)
    emit_L2_attn_rest(cx, W, xn_tm, battn_dram)


def emit_l2_tmproj(cx, x128, wx, psx, obuf, Xt):
    S = cx.S
    for tc in range(NTCH):
        for oh in range(4):
            p = psx[oh % 2]
            pk = f"PSB{oh % 2}"
            tiles = [(x128.get(ic, tc), [wx.get(ic, oh * 4 + j) for j in range(4)])
                     for ic in range(NDCH)]

            def fn(E, tiles=tiles, p=p):
                inst = None
                n = len(tiles)
                for i, ((x_ap, xk), ws) in enumerate(tiles):
                    for j, (w_ap, wk) in enumerate(ws):
                        inst = E.matmul(p[:, j * 128:(j + 1) * 128], x_ap, w_ap,
                                        start=(i == 0 and j == 0),
                                        stop=(i == n - 1 and j == len(ws) - 1))
                return inst
            rds = set()
            for (x_ap, xk), ws in tiles:
                rds.add(xk)
                rds.update(wk for _, wk in ws)
            S.op('pe', fn, reads=tuple(rds), writes=(pk,))
            S.op('act', lambda E, p=p: E.activation(obuf[:], p[:], AF.Copy),
                 reads=(pk,), writes=("l2x_ob",))
            S.dma('sp', Xt[tc * 128:(tc + 1) * 128, oh * 512:(oh + 1) * 512],
                  obuf[:], reads=("l2x_ob",))


def emit_L2_attn_rest(cx, W, xn_tm, battn_dram):
    S = cx.S
    sp, wp = cx.srcpool, cx.wpool
    Zf = cx.dram["l2_Z"]
    Xt = cx.dram["l2_Xt"]
    Bp, Cp, AD, TG = cx.dram["l2_Bp"], cx.dram["l2_Cp"], cx.dram["l2_AD"], cx.dram["l2_TG"]
    Yf, Gt, csD = cx.dram["l2_Y"], cx.dram["l2_G"], cx.dram["l2_cs"]
    bcpad = cx.dram["l2w_bcpad"]
    sp.reset(); wp.reset()
    xnsrc = FmSrc(cx, "l2xs", xn_tm, sp)
    wbcp = WTorch(cx, "l2wbcp", bcpad, 1024, 256, wp)
    wsm = WTorch(cx, "l2wsm", cx.dram["l2w_win"][4224:4432, :], 1024, 208, wp)
    smbuf = cx.sbuf("l2sm_ob", (128, 512), F32)
    psm = cx.psum("l2sm_ps", (128, 512), F32, bank=2)
    for tt in range(8):
        for wobj, oc, dst, nr in ((wbcp, 0, Bp, 128), (wbcp, 1, Cp, 128),
                                  (wsm, 0, AD, 128), (wsm, 1, TG, 80)):
            mms = [(wobj.get(ic, oc), xnsrc.get(ic, tt)) for ic in range(NDCH)]

            def fn2(E, mms=mms, nr=nr):
                inst = None
                for i, ((w_ap, wk), (x_ap, xk)) in enumerate(mms):
                    inst = E.matmul(psm[0:nr, :], w_ap[:, 0:nr], x_ap,
                                    start=(i == 0), stop=(i == len(mms) - 1))
                return inst
            rds = tuple(set([m[0][1] for m in mms] + [m[1][1] for m in mms]))
            S.op('pe', fn2, reads=rds, writes=("PSB2",))
            S.op('act', lambda E, nr=nr: E.activation(smbuf[0:nr, :], psm[0:nr, :],
                                                      AF.Copy),
                 reads=("PSB2",), writes=("l2sm_ob",))
            S.dma('sp', dst[0:nr, tt * 512:(tt + 1) * 512], smbuf[0:nr, :],
                  reads=("l2sm_ob",))

    bnw = cx.sbuf("l2_bnw", (128, 2), F32)
    S.op('dve', lambda E: E.memset(bnw[:], 0.0), writes=("l2_bnw",))
    for col, wname in ((0, "l2_Bn_w"), (1, "l2_Cn_w")):
        for dstrow, srow, n in ((0, 0, 16), (32, 16, 16), (64, 32, 32)):
            S.dma('sp', bnw[dstrow:dstrow + n, col:col + 1],
                  W[wname][srow:srow + n].unsqueeze(1),
                  reads=("l2_bnw",), writes=("l2_bnw",))
    maskc = cx.sbuf("l2_maskc", (128, 1), F32)
    S.op('dve', lambda E: E.memset(maskc[:], 0.0), writes=("l2_maskc",))
    for dstrow, n in ((0, 16), (32, 16), (64, 32)):
        S.op('dve', lambda E, dstrow=dstrow, n=n: E.memset(
                maskc[dstrow:dstrow + n, :], 1.0),
             reads=("l2_maskc",), writes=("l2_maskc",))
    dtb = cx.sbuf("l2_dtb", (64, 1), F32)
    S.dma('sp', dtb[:], W["l2_dt_bias"].unsqueeze(1), writes=("l2_dtb",))
    bct = cx.sbuf("l2_bct", (128, 512), F32)
    sqt = cx.sbuf("l2_sqt", (128, 512), F32)
    ms1 = cx.sbuf("l2_ms1", (1, 512), F32)
    rbc = cx.sbuf("l2_rbc", (128, 512), F32)
    rotbB = cx.sbuf("l2_rotbB", (96, 512), BF16)
    rotbC = cx.sbuf("l2_rotbC", (96, 512), BF16)
    S.op('dve', lambda E: E.memset(rotbB[:], 0.0), writes=("l2_rotbB",))
    S.op('dve', lambda E: E.memset(rotbC[:], 0.0), writes=("l2_rotbC",))
    rotbBD = cx.dram_scratch("l2_rotbBD", (96, S_LEN), BF16)
    rotbCD = cx.dram_scratch("l2_rotbCD", (96, S_LEN), BF16)
    cs65 = cx.sbuf("l2_cs65", (65, 512), BF16)
    cs65D = cx.dram_scratch("l2_cs65D", (65, S_LEN), BF16)
    S.op('dve', lambda E: E.memset(cs65[:], 0.0), writes=("l2_cs65",))
    psn1 = cx.psum("l2_psn", (1, 512), F32, bank=3)
    psb = cx.psum("l2_psb", (128, 512), F32, bank=4)
    tgt = cx.sbuf("l2_tgt", (80, 512), F32)
    angr = cx.sbuf("l2_angr", (16, 512), F32)
    angw = cx.sbuf("l2_angw", (16, 512), F32)
    angi = cx.sbuf("l2_angi", (16, 512), I32)
    trapS = cx.sbuf("l2_trap", (64, 512), F32)
    dtS = cx.sbuf("l2_dt", (64, 512), F32)
    aS = cx.sbuf("l2_aS", (64, 512), F32)
    adt = cx.sbuf("l2_adt", (128, 512), F32)
    tA = cx.sbuf("l2_tA", (64, 512), F32)
    tB = cx.sbuf("l2_tB", (64, 512), F32)
    lnaS = cx.sbuf("l2_lna", (64, 512), F32)
    csS = cx.sbuf("l2_csS", (64, 512), F32)
    cscarry = cx.sbuf("l2_cscarry", (64, 1), F32)
    prevcarry = cx.sbuf("l2_prevcarry", (64, 1), F32)
    gdtS = cx.sbuf("l2_gdt", (64, 512), F32)
    ncsT = cx.sbuf("l2_ncsT", (64, 512), F32)
    dfb = cx.sbuf("l2_dfb", (64, 1), F32)
    negcsD = cx.dram_scratch("l2_negcs", (64, S_LEN), F32)
    wdecS = cx.sbuf("l2_wdec", (64, 512), F32)
    gdtT = cx.sbuf("l2_gdtT", (128, NTCH * 64), F32)
    dfT = cx.sbuf("l2_dfT", (128, NTCH * 64), F32)
    wdecT = cx.sbuf("l2_wdecT", (128, NTCH * 64), F32)
    dff = cx.sbuf("l2_dff", (64, 128), F32)
    psw = cx.psum("l2_psw", (128, 64), F32, bank=5)
    zeros64 = cx.sbuf("l2_z64", (64, 512), BF16)
    epc1 = cx.constcol(1e-5, 1)
    ang64 = cx.sbuf("l2_ang64", (16, 512), F32)
    csB0, rbcB0 = ang64, angw
    for tt in range(8):
        sl = slice(tt * 512, (tt + 1) * 512)
        S.dma('sp', tgt[:], TG[:, sl], writes=("l2_tgt",))
        S.op('dve', lambda E: E.tensor_copy(ang64[:], tgt[64:80, :]),
             reads=("l2_tgt",), writes=("l2_ang64",))
        for which, shiftpi, row in (("cos", 0.5 * math.pi, 0), ("sin", 0.0, 32)):
            S.op('dve', lambda E, shiftpi=shiftpi: E.tensor_scalar(
                    angr[:], ang64[:], float(1.0 / (2 * math.pi)),
                    float(shiftpi / (2 * math.pi)), ALU.mult, ALU.add),
                 reads=("l2_ang64",), writes=("l2_angr",))
            S.op('dve', lambda E: E.tensor_copy(angi[:], angr[:]),
                 reads=("l2_angr",), writes=("l2_angi",))
            S.op('dve', lambda E: E.tensor_copy(angr[:], angi[:]),
                 reads=("l2_angi",), writes=("l2_angr",))
            S.op('dve', lambda E: E.scalar_tensor_tensor(
                    angr[:], angr[:], float(-2 * math.pi), ang64[:],
                    ALU.mult, ALU.add),
                 reads=("l2_angr", "l2_ang64"), writes=("l2_angr",))
            if shiftpi:
                S.op('dve', lambda E, shiftpi=shiftpi: E.tensor_single_scalar(
                        angr[:], angr[:], float(shiftpi), ALU.add),
                     reads=("l2_angr",), writes=("l2_angr",))
            S.op('dve', lambda E: E.tensor_single_scalar(angw[:], angr[:],
                                                         float(math.pi), ALU.is_gt),
                 reads=("l2_angr",), writes=("l2_angw",))
            S.op('dve', lambda E: E.scalar_tensor_tensor(
                    angr[:], angw[:], float(-2 * math.pi), angr[:],
                    ALU.mult, ALU.add),
                 reads=("l2_angw", "l2_angr"), writes=("l2_angr",))
            S.op('dve', lambda E: E.tensor_single_scalar(angw[:], angr[:],
                                                         float(-math.pi), ALU.is_lt),
                 reads=("l2_angr",), writes=("l2_angw",))
            S.op('dve', lambda E: E.scalar_tensor_tensor(
                    angr[:], angw[:], float(2 * math.pi), angr[:],
                    ALU.mult, ALU.add),
                 reads=("l2_angw", "l2_angr"), writes=("l2_angr",))
            S.op('dve', lambda E: E.tensor_single_scalar(angr[:], angr[:],
                                                         float(math.pi), ALU.min),
                 reads=("l2_angr",), writes=("l2_angr",))
            S.op('dve', lambda E: E.tensor_single_scalar(angr[:], angr[:],
                                                         float(-math.pi), ALU.max),
                 reads=("l2_angr",), writes=("l2_angr",))
            S.op('act', lambda E, row=row: E.activation(
                    cs65[row:row + 16, :], angr[:], AF.Sin),
                 reads=("l2_angr",), writes=("l2_cs65",))
        S.op('dve', lambda E: E.memset(cs65[64:65, :], 1.0),
             reads=(), writes=("l2_cs65",))
        S.dma('sp', cs65D[:, sl], cs65[:], reads=("l2_cs65",))
        for half, (srcd, col) in enumerate(((Bp, 0), (Cp, 1))):
            S.dma('sp', bct[:], srcd[:, sl], writes=("l2_bct",))
            S.op('act', lambda E: E.activation(sqt[:], bct[:], AF.Square),
                 reads=("l2_bct",), writes=("l2_sqt",))
            S.op('pe', lambda E: E.matmul(psn1[:], maskc[:], sqt[:],
                                          start=True, stop=True),
                 reads=("l2_sqt", "l2_maskc"), writes=("PSB3",))
            S.op('act', lambda E, epc1=epc1: E.activation(
                    ms1[:], psn1[:], AF.Ln, scale=1.0 / 64, bias=epc1[:]),
                 reads=("PSB3", f"ccol{(1e-05, 1)}"), writes=("l2_ms1",))
            S.op('act', lambda E: E.activation(ms1[:], ms1[:], AF.Exp, scale=-0.5),
                 reads=("l2_ms1",), writes=("l2_ms1",))
            S.op('pe', lambda E: E.matmul(psb[:], cx.sb["ones_row_f"][:], ms1[:],
                                          start=True, stop=True),
                 reads=("l2_ms1", "c_ones_f"), writes=("PSB4",))
            S.op('act', lambda E: E.activation(rbc[:], psb[:], AF.Copy),
                 reads=("PSB4",), writes=("l2_rbc",))
            S.op('dve', lambda E: E.tensor_mul(rbc[:], rbc[:], bct[:]),
                 reads=("l2_rbc", "l2_bct"), writes=("l2_rbc",))
            S.op('dve', lambda E, col=col: E.tensor_scalar(
                    rbc[:], rbc[:], bnw[:, col:col + 1], None, ALU.mult),
                 reads=("l2_rbc", "l2_bnw"), writes=("l2_rbc",))
            rb_t = rotbB if half == 0 else rotbC
            rbk = "l2_rotbB" if half == 0 else "l2_rotbC"
            S.op('dve', lambda E: E.tensor_mul(tA[0:16, :], rbc[0:16, :],
                                               cs65[0:16, :]),
                 reads=("l2_rbc", "l2_cs65"), writes=("l2_tA",))
            S.op('dve', lambda E: E.tensor_mul(tB[0:16, :], rbc[32:48, :],
                                               cs65[32:48, :]),
                 reads=("l2_rbc", "l2_cs65"), writes=("l2_tB",))
            S.op('dve', lambda E, rb_t=rb_t: E.tensor_sub(
                    rb_t[0:16, :], tA[0:16, :], tB[0:16, :]),
                 reads=("l2_tA", "l2_tB"), writes=(rbk,))
            S.op('dve', lambda E: E.tensor_copy(csB0[:], cs65[32:48, :]),
                 reads=("l2_cs65",), writes=("l2_ang64",))
            S.op('dve', lambda E: E.tensor_copy(rbcB0[:], rbc[32:48, :]),
                 reads=("l2_rbc",), writes=("l2_angw",))
            S.op('dve', lambda E: E.tensor_mul(tA[0:16, :], rbc[0:16, :],
                                               csB0[:]),
                 reads=("l2_rbc", "l2_ang64"), writes=("l2_tA",))
            S.op('dve', lambda E: E.tensor_mul(tB[0:16, :], rbcB0[:],
                                               cs65[0:16, :]),
                 reads=("l2_angw", "l2_cs65"), writes=("l2_tB",))
            S.op('dve', lambda E, rb_t=rb_t: E.tensor_add(
                    rb_t[32:48, :], tA[0:16, :], tB[0:16, :]),
                 reads=("l2_tA", "l2_tB"), writes=(rbk,))
            S.op('dve', lambda E, rb_t=rb_t: E.tensor_copy(
                    rb_t[64:96, :], rbc[64:96, :]),
                 reads=("l2_rbc",), writes=(rbk,))
            S.dma('sp', (rotbBD if half == 0 else rotbCD)[:, sl], rb_t[:],
                  reads=(rbk,))
        S.dma('sp', adt[:], AD[:, sl], writes=("l2_adt",))
        S.op('dve', lambda E, tt=tt: E.tensor_scalar(dtS[:], adt[0:64, :],
                                                     dtb[:], None, ALU.add),
             reads=("l2_adt", "l2_dtb"), writes=("l2_pre",))
        emit_softplus(cx, "l2dt", dtS[:], dtS[:], "l2_pre", "l2_dtS", (64, 512))
        emit_softplus(cx, "l2A", aS[:], adt[64:128, :], "l2_adt", "l2_aSsp", (64, 512))
        S.op('dve', lambda E: E.tensor_single_scalar(aS[:], aS[:], 1e-4, ALU.max),
             reads=("l2_aSsp",), writes=("l2_aSsp",))
        S.op('dve', lambda E, tt=tt: E.scalar_tensor_tensor(
                aS[:], aS[:], -1.0, dtS[:], ALU.mult, ALU.mult),
             reads=("l2_aSsp", "l2_dtS"), writes=("l2_aexp",))
        S.op('act', lambda E: E.activation(trapS[:], tgt[0:64, :], AF.Sigmoid),
             reads=("l2_tgt",), writes=("l2_trapS",))
        S.op('act', lambda E: E.activation(aS[:], aS[:], AF.Exp),
             reads=("l2_aexp",), writes=("l2_aexp2",))
        S.op('dve', lambda E, tt=tt: E.tensor_scalar(gdtS[:], trapS[:],
                                                     -0.5, 1.0, ALU.mult, ALU.add),
             reads=("l2_trapS",), writes=("l2_gdtS",))
        S.op('dve', lambda E, tt=tt: E.tensor_mul(aS[:], aS[:], gdtS[:]),
             reads=("l2_aexp2", "l2_gdtS"), writes=("l2_aexp2",))
        S.op('dve', lambda E, tt=tt: E.scalar_tensor_tensor(
                aS[:], trapS[:], 0.5, aS[:], ALU.mult, ALU.add),
             reads=("l2_trapS", "l2_aexp2"), writes=("l2_afin",))
        S.op('act', lambda E, tt=tt: E.activation(lnaS[:], aS[:], AF.Ln),
             reads=("l2_afin",), writes=("l2_lnaS",))
        S.op('dve', lambda E, tt=tt: E.tensor_mul(gdtS[:], gdtS[:], dtS[:]),
             reads=("l2_gdtS", "l2_dtS"), writes=("l2_gdtS",))
        if tt > 0:
            S.op('dve', lambda E: E.tensor_copy(prevcarry[:], cscarry[:]),
                 reads=("l2_cscarry",), writes=("l2_prevcarry",))
        if tt == 0:
            S.op('dve', lambda E: E.memset(zeros64[:], 0.0), writes=("l2_z64",))
            S.op('dve', lambda E: E.tensor_tensor_scan(
                    csS[:], zeros64[:], lnaS[:], 0.0, ALU.add, ALU.add),
                 reads=("l2_z64", "l2_lnaS"), writes=("l2_csS",))
        else:
            S.op('dve', lambda E: E.tensor_tensor_scan(
                    csS[:], zeros64[:], lnaS[:], cscarry[:], ALU.add, ALU.add),
                 reads=("l2_z64", "l2_lnaS", "l2_cscarry"), writes=("l2_csS",))
        S.op('dve', lambda E: E.tensor_copy(cscarry[:], csS[:, 511:512]),
             reads=("l2_csS",), writes=("l2_cscarry",))
        S.dma('sp', csD[:, sl], csS[:], reads=("l2_csS",))
        S.op('dve', lambda E: E.tensor_single_scalar(
                ncsT[:], csS[:], -1.0, ALU.mult),
             reads=("l2_csS",), writes=("l2_ncsT",))
        S.dma('sp', negcsD[:, sl], ncsT[:], reads=("l2_ncsT",))
        for q in range(4):
            c = tt * 4 + q
            qsl = slice(q * 128, (q + 1) * 128)
            S.op('pe', lambda E, qsl=qsl: E.transpose(
                    psw[:], gdtS[:, qsl], cx.sb["ident_f"][0:64, 0:64]),
                 reads=("l2_gdtS", "c_ident_f"), writes=("PSB5",))
            S.op('act', lambda E, c=c: E.activation(gdtT[:, c * 64:(c + 1) * 64],
                                                    psw[:], AF.Copy),
                 reads=("PSB5",), writes=("l2_gdtT",))
            ecol = csS[:, q * 128 + 127:q * 128 + 128]
            S.op('act', lambda E, qsl=qsl, ecol=ecol: E.activation(
                    wdecS[:, 0:128], csS[:, qsl], AF.Exp,
                    scale=-1.0, bias=ecol),
                 reads=("l2_csS",), writes=("l2_wdec",))
            S.op('dve', lambda E, qsl=qsl: E.tensor_mul(
                    wdecS[:, 0:128], wdecS[:, 0:128], gdtS[:, qsl]),
                 reads=("l2_wdec", "l2_gdtS"), writes=("l2_wdec",))
            S.op('pe', lambda E: E.transpose(psw[:], wdecS[:, 0:128],
                                             cx.sb["ident_f"][0:64, 0:64]),
                 reads=("l2_wdec", "c_ident_f"), writes=("PSB5",))
            S.op('act', lambda E, c=c: E.activation(wdecT[:, c * 64:(c + 1) * 64],
                                                    psw[:], AF.Copy),
                 reads=("PSB5",), writes=("l2_wdecT",))
            if c == 0:
                S.op('dve', lambda E: E.memset(dfb[:], 0.0), writes=("l2_dfb",))
            elif q == 0:
                S.op('dve', lambda E: E.tensor_single_scalar(
                        dfb[:], prevcarry[:], -1.0, ALU.mult),
                     reads=("l2_prevcarry",), writes=("l2_dfb",))
            else:
                S.op('dve', lambda E, q=q: E.tensor_single_scalar(
                        dfb[:], csS[:, q * 128 - 1:q * 128], -1.0, ALU.mult),
                     reads=("l2_csS",), writes=("l2_dfb",))
            S.op('act', lambda E, qsl=qsl: E.activation(
                    dff[:], csS[:, qsl], AF.Exp, bias=dfb[:]),
                 reads=("l2_csS", "l2_dfb"), writes=("l2_dff",))
            S.op('pe', lambda E: E.transpose(psw[:], dff[:],
                                             cx.sb["ident_f"][0:64, 0:64]),
                 reads=("l2_dff", "c_ident_f"), writes=("PSB5",))
            S.op('act', lambda E, c=c: E.activation(dfT[:, c * 64:(c + 1) * 64],
                                                    psw[:], AF.Copy),
                 reads=("PSB5",), writes=("l2_dfT",))

    wp.reset()
    wcarve, wck = wp.alloc(64)
    WbD = cx.dram_scratch("l2_WbD", (65, 8192), BF16)
    WcD = cx.dram_scratch("l2_WcD", (65, 8192), BF16)
    for bias_nm, wb_nm, wdst in (("l2_B_bias", "l2_Wb", WbD),
                                 ("l2_C_bias", "l2_Wc", WcD)):
        negD = cx.dram_scratch(f"{wb_nm}_neg", (64, 64), F32)
        nb64 = cx.sbuf("l2_nb64", (64, 64), F32)
        S.dma('sp', nb64[:], W[bias_nm], writes=("l2_nb64",))
        S.op('dve', lambda E: E.tensor_single_scalar(nb64[:], nb64[:], -1.0, ALU.mult),
             reads=("l2_nb64",), writes=("l2_nb64",))
        S.dma('sp', negD, nb64[:], reads=("l2_nb64",))
        wbt = wcarve[0:65, 0:8192]
        wbkey = "l2_Wbt_buf"
        S.op('dve', lambda E, wbt=wbt: E.memset(wbt[:], 0.0), writes=(wbkey,))

        def strided_w(row, o, src2d, rds=()):
            dst = wbt[row:row + 1, :].rearrange("p (h o) -> p o h", o=128)[:, o:o + 1, :]
            S.dma('pool', dst, src2d.unsqueeze(1), reads=rds, writes=(wbkey,),
                  allow_slow_non_contiguous=True)
        bsrc = W[bias_nm]
        for n in range(16):
            strided_w(n, n, bsrc[:, n:n + 1].rearrange("h o -> o h"))
            strided_w(n, 32 + n, bsrc[:, 16 + n:16 + n + 1].rearrange("h o -> o h"))
            strided_w(32 + n, n, negD[:, 16 + n:16 + n + 1].rearrange("h o -> o h"),
                      rds=(f"D_{wb_nm}_neg",))
            strided_w(32 + n, 32 + n, bsrc[:, n:n + 1].rearrange("h o -> o h"))
        dst = wbt[64:65, :].rearrange("p (h o) -> p h o", o=128)[:, :, 64:96]
        S.dma('pool', dst, bsrc[:, 32:64].unsqueeze(0), writes=(wbkey,))
        S.dma('sp', wdst, wbt[:], reads=(wbkey,))
    rotbBt = cx.sbuf("l2_rotbBt", (128, NTCH * 96), BF16)
    rotbf16 = cx.sbuf("l2_rotbf16", (96, 128), BF16)
    rotbf = cx.sbuf("l2_rotbf", (96, 128), F32)
    pstr = cx.psum("l2_pstr", (128, 96), F32, bank=5)
    for c in range(NTCH):
        S.dma('sp', rotbf16[:], rotbBD[:, c * 128:(c + 1) * 128],
              writes=("l2_rotbf16",))
        S.op('dve', lambda E: E.tensor_copy(rotbf[:], rotbf16[:]),
             reads=("l2_rotbf16",), writes=("l2_rotbf",))
        S.op('pe', lambda E: E.transpose(pstr[:], rotbf[:],
                                         cx.sb["ident_f"][0:96, 0:96]),
             reads=("l2_rotbf", "c_ident_f"), writes=("PSB5",))
        S.op('act', lambda E, c=c: E.activation(
                rotbBt[:, c * 96:(c + 1) * 96], pstr[:], AF.Copy),
             reads=("PSB5",), writes=("l2_rotbBt",))
    dcol = cx.sbuf("l2_dcol", (1, 64), F32)
    S.dma('sp', dcol[:], W["l2_D"].unsqueeze(0), writes=("l2_dcol",))
    psdc = cx.psum("l2_psdc", (128, 64), F32, bank=5)
    Dcst = cx.sbuf("l2_Dcst", (128, 64), F32)
    S.op('pe', lambda E: E.matmul(psdc[:], cx.sb["ones_row_f"][:], dcol[:],
                                  start=True, stop=True),
         reads=("l2_dcol", "c_ones_f"), writes=("PSB5",))
    S.op('act', lambda E: E.activation(Dcst[:], psdc[:], AF.Copy),
         reads=("PSB5",), writes=("l2_Dcst",))
    ends = cx.sbuf("l2_ends", (64, 32), F32)
    dtot = cx.sbuf("l2_dtot", (64, 32), F32)
    S.dma('sp', ends[:], csD.rearrange("h (c s) -> h c s", s=128)[:, :, 127],
          writes=("l2_ends",), allow_slow_non_contiguous=True)
    S.op('dve', lambda E: E.tensor_sub(dtot[:, 1:32], ends[:, 1:32], ends[:, 0:31]),
         reads=("l2_ends",), writes=("l2_dtot",))
    S.op('dve', lambda E: E.tensor_copy(dtot[:, 0:1], ends[:, 0:1]),
         reads=("l2_ends",), writes=("l2_dtot",))
    S.op('act', lambda E: E.activation(dtot[:], dtot[:], AF.Exp),
         reads=("l2_dtot",), writes=("l2_dtot",))
    wp.reset()
    rep32f_c, rep32f_k = wp.alloc(16)
    rep32f = rep32f_c[0:64, :]
    S.alias["l2_rep32f"] = rep32f_k[0]
    S.dma('sp', rep32f[:], cx.consts["rep32full"], writes=("l2_rep32f",))
    dtb16 = cx.sbuf("l2_dtot16", (64, 32), BF16)
    S.op('dve', lambda E: E.tensor_copy(dtb16[:], dtot[:]),
         reads=("l2_dtot",), writes=("l2_dtot16",))
    decrepD = cx.dram_scratch("l2_decrep", (32, 2048), F32)
    psd = cx.psum("l2_psd", (32, 512), F32, bank=5)
    drb = cx.sbuf("l2sm_ob", (128, 512), F32)
    for j in range(4):
        S.op('pe', lambda E, j=j: E.matmul(psd[:], dtb16[:],
                                           rep32f[:, j * 512:(j + 1) * 512],
                                           start=True, stop=True),
             reads=("l2_dtot16", "l2_rep32f"), writes=("PSB5",))
        S.op('act', lambda E: E.activation(drb[0:32, :], psd[:], AF.Copy),
             reads=("PSB5",), writes=("l2sm_ob",))
        S.dma('sp', decrepD[:, j * 512:(j + 1) * 512], drb[0:32, :],
              reads=("l2sm_ob",))
    sp.reset(); wp.reset()
    xc, kxc = sp.alloc(4)
    xg, kxg = sp.alloc(4)
    statef, kst = sp.allocf32(8)
    state = statef[0:96, :]
    state16f, kst16 = sp.alloc(4)
    state16 = state16f[0:96, :]
    bmt, kbmt = sp.alloc(1)
    bmf, kbmf = sp.alloc(1)
    cmf, kcmf = sp.alloc(1)
    msb, kmsb = sp.alloc(1)
    lsbf, klsb = wp.allocf32(8)
    lsb = lsbf[:, 0:512]
    csrf, kcsr = wp.allocf32(8)
    csr = csrf[0:1, 0:512]
    ncsrf, kncsr = wp.allocf32(8)
    ncsr = ncsrf[0:1, 0:512]
    ytf, kyt = wp.allocf32(2)
    ytile = ytf[:, 0:128]
    drow = cx.sbuf("l2c_drow", (1, 128), F32)
    itile = cx.sbuf("l2c_it", (128, 128), F32)
    yts = cx.sbuf("l2c_yts", (128, 128), BF16)
    st1 = cx.sbuf("l2c_st1", (96, 128), F32)
    onesr96 = cx.sbuf("ones_row_f", (1, 128), F32)
    kxc, kxg = kxc[0], kxg[0]
    kst, kst16 = kst[0], kst16[0]
    kbmt, kbmf, kcmf, kmsb = kbmt[0], kbmf[0], kcmf[0], kmsb[0]
    klsb, kcsr, kncsr, kyt = klsb[0], kcsr[0], kncsr[0], kyt[0]
    KMAP = {"l2c_xc": kxc, "l2c_xg": kxg, "l2c_state": kst, "l2c_state16": kst16,
            "l2c_bmt": kbmt, "l2c_bmf": kbmf, "l2c_cmf": kcmf, "l2c_msb": kmsb,
            "l2c_lsb": klsb, "l2c_csr": kcsr, "l2c_ncsr": kncsr, "l2c_y": kyt}
    for _k, _v in KMAP.items():
        S.alias[_k] = _v
    S.op('dve', lambda E: E.memset(state[:], 0.0), writes=("l2c_state",))
    S.op('dve', lambda E: E.memset(state16[:], 0.0), writes=("l2c_state16",))
    psA_ = cx.psum("l2c_psA", (128, 512), F32, bank=0)
    psB_ = cx.psum("l2c_psB", (128, 512), F32, bank=1)
    psC_ = cx.psum("l2c_psC", (128, 512), F32, bank=2)
    psG_ = cx.psum("l2c_psG", (128, 512), F32, bank=0)
    psL_ = cx.psum("l2c_psL", (128, 512), F32, bank=3)
    psY_ = cx.psum("l2c_psY", (128, 128), F32, bank=4)
    psI_ = cx.psum("l2c_psI", (128, 128), F32, bank=7)
    psS_ = cx.psum("l2c_psS", (96, 128), F32, bank=6)
    psD_ = cx.psum("l2c_psD", (96, 128), F32, bank=5)
    identb = cx.sb["ident_b"]
    cs65c = cx.sbuf("l2c_cs65", (65, 128), BF16)
    rbBc = cx.sbuf("l2c_rbB", (128, 128), BF16)
    rbCc = cx.sbuf("l2c_rbC", (128, 128), BF16)
    S.op('dve', lambda E: E.memset(rbBc[:], 0.0), writes=("l2c_rbB",))
    S.op('dve', lambda E: E.memset(rbCc[:], 0.0), writes=("l2c_rbC",))
    wbg = [cx.sbuf(f"l2c_wbg{i}", (65, 512), BF16) for i in range(2)]
    for c in range(NTCH):
        S.dma('sp', xc[:], Xt[c * 128:(c + 1) * 128, :], writes=("l2c_xc",))
        S.dma('sp', cs65c[:], cs65D[:, c * 128:(c + 1) * 128], writes=("l2c_cs65c",))
        S.dma('sp', rbBc[0:96, :], rotbBD[:, c * 128:(c + 1) * 128],
              writes=("l2c_rbB",))
        S.dma('sp', rbCc[0:96, :], rotbCD[:, c * 128:(c + 1) * 128],
              writes=("l2c_rbC",))
        for h in range(64):
            S.op('dve', lambda E, h=h, c=c: E.tensor_scalar(
                    xg[:, h * 32:(h + 1) * 32], xc[:, h * 32:(h + 1) * 32],
                    gdtT[:, c * 64 + h:c * 64 + h + 1], None, ALU.mult),
                 reads=("l2c_xc", "l2_gdtT"), writes=("l2c_xg",))
        for g4 in range(16):
            h0 = g4 * 4
            S.dma('sp', wbg[0][:], WbD[:, g4 * 512:(g4 + 1) * 512],
                  writes=("l2c_wbg0",))
            S.dma('sp', wbg[1][:], WcD[:, g4 * 512:(g4 + 1) * 512],
                  writes=("l2c_wbg1",))

            def fn_bmt(E, c=c, g4=g4):
                inst = E.matmul(psA_[:], cs65c[:], wbg[0][:], start=True, stop=False)
                rb = rotbBt[:, c * 96:(c + 1) * 96]
                for j in range(4):
                    inst = E.matmul(psA_[:, j * 128:j * 128 + 96], identb[:], rb,
                                    start=False, stop=(j == 3))
                return inst
            S.op('pe', fn_bmt, reads=("l2c_cs65c", "l2c_wbg0", "l2_rotbBt",
                                      "c_ident_b"),
                 writes=("PSB0",))
            S.op('act', lambda E: E.activation(bmt[:], psA_[:], AF.Copy),
                 reads=("PSB0",), writes=("l2c_bmt",))
            for j in range(4):
                S.op('dve', lambda E, j=j, c=c, h0=h0: E.tensor_scalar(
                        bmt[:, j * 128:j * 128 + 96], bmt[:, j * 128:j * 128 + 96],
                        wdecT[:, c * 64 + h0 + j:c * 64 + h0 + j + 1], None, ALU.mult),
                     reads=("l2c_bmt", "l2_wdecT"), writes=("l2c_bmt",))
            for (wi, wk, rsrc, rk, ps, pk, dst, dk) in (
                    (0, "l2c_wbg0", rbBc, "l2c_rbB", psB_, "PSB1", bmf, "l2c_bmf"),
                    (1, "l2c_wbg1", rbCc, "l2c_rbC", psC_, "PSB2", cmf, "l2c_cmf")):
                def fn_fm(E, wi=wi, rsrc=rsrc, ps=ps):
                    inst = None
                    for j in range(4):
                        E.matmul(ps[:, j * 128:(j + 1) * 128],
                                 wbg[wi][:, j * 128:(j + 1) * 128],
                                 cs65c[:], start=(j == 0), stop=False)
                        inst = E.matmul(ps[:, j * 128:(j + 1) * 128],
                                        identb[:], rsrc[:],
                                        start=False, stop=(j == 3))
                    return inst
                S.op('pe', fn_fm, reads=(wk, rk, "l2c_cs65c", "c_ident_b"),
                     writes=(pk,))
                S.op('act', lambda E, ps=ps, dst=dst: E.activation(dst[:], ps[:],
                                                                   AF.Copy),
                     reads=(pk,), writes=(dk,))
            def fn_g(E):
                inst = None
                for j in range(4):
                    inst = E.matmul(psG_[:, j * 128:(j + 1) * 128],
                                    bmf[0:96, j * 128:(j + 1) * 128],
                                    cmf[0:96, j * 128:(j + 1) * 128],
                                    start=(j == 0), stop=(j == 3))
                return inst
            S.op('pe', fn_g, reads=("l2c_bmf", "l2c_cmf"), writes=("PSB0",))
            S.dma('sp', csr.rearrange("p (h s) -> p h s", h=4),
                  csD[h0:h0 + 4, c * 128:(c + 1) * 128].unsqueeze(0),
                  writes=("l2c_csr",))
            S.dma('sp', ncsr.rearrange("p (h s) -> p h s", h=4),
                  negcsD[h0:h0 + 4, c * 128:(c + 1) * 128].unsqueeze(0),
                  writes=("l2c_ncsr",))

            def fn_l(E):
                for j in range(4):
                    E.matmul(psL_[:, j * 128:(j + 1) * 128], onesr96[:, 0:128],
                             csr[:, j * 128:(j + 1) * 128], start=(j == 0),
                             stop=False)
                    E.matmul(psL_[:, j * 128:(j + 1) * 128],
                             ncsr[:, j * 128:(j + 1) * 128],
                             cx.sb["ones_row_f"][:, 0:128], start=False, stop=False)
                return E.matmul(psL_[:], identb[:], cx.sb["trineg4"][:],
                                start=False, stop=True)
            S.op('pe', fn_l, reads=("l2c_csr", "l2c_ncsr", "c_ones_f", "c_trineg4",
                                    "c_ident_b"), writes=("PSB3",))
            S.op('act', lambda E: E.activation(lsb[:], psL_[:], AF.Exp),
                 reads=("PSB3",), writes=("l2c_lsb",))
            S.op('dve', lambda E: E.tensor_mul(msb[:], lsb[:], psG_[:]),
                 reads=("l2c_lsb", "PSB0"), writes=("l2c_msb",))
            def fn_y(E, c=c, g4=g4, h0=h0):
                inst = None
                for j in range(4):
                    inst = E.matmul(psY_[:, 32 * j:32 * (j + 1)],
                                    msb[:, j * 128:(j + 1) * 128],
                                    xg[:, (h0 + j) * 32:(h0 + j + 1) * 32],
                                    start=(j == 0), stop=(j == 3))
                return inst
            S.op('pe', fn_y, reads=("l2c_xg", "l2c_msb"), writes=("PSB4",))

            def fn_i(E, c=c, g4=g4, h0=h0):
                inst = None
                for j in range(4):
                    inst = E.matmul(psI_[:, 32 * j:32 * (j + 1)],
                                    cmf[0:96, j * 128:(j + 1) * 128],
                                    state16[:, (h0 + j) * 32:(h0 + j + 1) * 32],
                                    start=(j == 0), stop=(j == 3))
                return inst
            S.op('pe', fn_i, reads=("l2c_cmf", "l2c_state16"), writes=("PSB7",))
            S.op('act', lambda E: E.activation(itile[:], psI_[:], AF.Copy),
                 reads=("PSB7",), writes=("l2c_it",))
            for j in range(4):
                S.op('dve', lambda E, j=j, c=c, h0=h0: E.tensor_scalar(
                        itile[:, 32 * j:32 * (j + 1)], itile[:, 32 * j:32 * (j + 1)],
                        dfT[:, c * 64 + h0 + j:c * 64 + h0 + j + 1], None, ALU.mult),
                     reads=("l2c_it", "l2_dfT"), writes=("l2c_it",))
            S.op('act', lambda E: E.activation(ytile[:], psY_[:], AF.Copy),
                 reads=("PSB4",), writes=("l2c_y",))
            if not os.environ.get("L2_NO_INTER"):
                S.op('dve', lambda E: E.tensor_add(ytile[:], ytile[:], itile[:]),
                     reads=("l2c_y", "l2c_it"), writes=("l2c_y",))
            for j in range(4):
                S.op('dve', lambda E, j=j, h0=h0: E.scalar_tensor_tensor(
                        yts[:, 32 * j:32 * (j + 1)],
                        xc[:, (h0 + j) * 32:(h0 + j + 1) * 32],
                        Dcst[:, h0 + j:h0 + j + 1],
                        ytile[:, 32 * j:32 * (j + 1)], ALU.mult, ALU.add),
                     reads=("l2c_xc", "l2_Dcst", "l2c_y"), writes=("l2c_yts",))
            S.dma('sp', Yf[c * 128:(c + 1) * 128, g4 * 128:(g4 + 1) * 128], yts[:],
                  reads=("l2c_yts",))
            def fn_s(E, h0=h0):
                inst = None
                for j in range(4):
                    inst = E.matmul(psS_[:, j * 32:(j + 1) * 32],
                                    bmt[:, j * 128:j * 128 + 96],
                                    xc[:, (h0 + j) * 32:(h0 + j + 1) * 32],
                                    start=(j == 0), stop=(j == 3))
                return inst
            S.op('pe', fn_s, reads=("l2c_bmt", "l2c_xc"), writes=("PSB6",))
            if os.environ.get("L2_TAP_STATE") and c == 0 and g4 in (0, 1):
                S.dma('sp', cx.dram_scratch(f"dbg_bmt_{g4}", (128, 512), BF16)[:, :],
                      bmt[:], reads=("l2c_bmt",))
                _dsb = cx.sbuf("dbg_psS_sb", (96, 128), F32)
                S.op('act', lambda E, _dsb=_dsb: E.activation(
                        _dsb[:], psS_[:], AF.Copy),
                     reads=("PSB6",), writes=("dbg_psS_sb",))
                S.dma('sp', cx.dram_scratch(f"dbg_psS_{g4}", (96, 128), F32)[:, :],
                      _dsb[:], reads=("dbg_psS_sb",))
            S.dma('sp', drow[:], decrepD[c:c + 1, g4 * 128:(g4 + 1) * 128],
                  writes=("l2c_drow",))
            S.op('pe', lambda E: E.matmul(
                    psD_[:], onesr96[:, 0:96], drow[:],
                    start=True, stop=True),
                 reads=("l2c_drow", "c_ones_f"), writes=("PSB5",))
            S.op('dve', lambda E, g4=g4: E.tensor_mul(
                    st1[:], state[:, g4 * 128:(g4 + 1) * 128], psD_[:]),
                 reads=("l2c_state", "PSB5"), writes=("l2c_st1",))
            S.op('dve', lambda E, g4=g4: E.tensor_add(
                    state[:, g4 * 128:(g4 + 1) * 128], st1[:], psS_[:]),
                 reads=("l2c_st1", "PSB6"), writes=("l2c_state",))
            S.op('dve', lambda E, g4=g4: E.tensor_copy(
                    state16[:, g4 * 128:(g4 + 1) * 128],
                    state[:, g4 * 128:(g4 + 1) * 128]),
                 reads=("l2c_state",), writes=("l2c_state16",))
            if os.environ.get("L2_TAP_STATE") and c == 1 and g4 == 0:
                S.dma('sp', cx.dram_scratch("dbg_it1", (128, 128), F32)[:, :],
                      itile[:], reads=("l2c_it",))
        if os.environ.get("L2_TAP_STATE") and c in (0, NTCH - 1):
            S.dma('sp', cx.dram_scratch(f"dbg_st_{c}", (96, 2048), F32)[:, :],
                  state[:], reads=("l2c_state",))
            S.dma('sp', cx.dram_scratch(f"dbg_st16_{c}", (96, 2048), BF16)[:, :],
                  state16[:], reads=("l2c_state16",))

    sp.reset(); wp.reset()
    yt2, kyt2 = sp.alloc(1)
    zt2, kzt2 = sp.alloc(1)
    gt2, kgt2 = sp.alloc(1)
    for _k, _v in (("l2g_y", kyt2[0]), ("l2g_z", kzt2[0]), ("l2g_g", kgt2[0])):
        S.alias[_k] = _v
    for tc in range(NTCH):
        for oh in range(4):
            S.dma('sp', yt2[:], Yf[tc * 128:(tc + 1) * 128, oh * 512:(oh + 1) * 512],
                  writes=("l2g_y",))
            S.dma('sp', zt2[:], Zf[tc * 128:(tc + 1) * 128, oh * 512:(oh + 1) * 512],
                  writes=("l2g_z",))
            S.op('act', lambda E: E.activation(zt2[:], zt2[:], AF.Silu),
                 reads=("l2g_z",), writes=("l2g_z",))
            S.op('dve', lambda E: E.tensor_mul(gt2[:], yt2[:], zt2[:]),
                 reads=("l2g_y", "l2g_z"), writes=("l2g_g",))
            S.dma('sp', Gt[tc * 128:(tc + 1) * 128, oh * 512:(oh + 1) * 512],
                  gt2[:], reads=("l2g_g",))
    sp.reset(); wp.reset()
    gsrc = FmSrc(cx, "l2gs", Gt, sp, n_ic=16)
    wo2 = WTorch(cx, "l2wo", cx.dram["l2w_wout"], 2048, 1024, wp)
    emit_dense_fm(cx, "l2o", wo2.get, gsrc.get, 16, NDCH,
                  sink_to_dram(cx, "l2o", battn_dram, dtype=F32, pool=sp))



def emit_L2_ffn(cx, W, xn_tm, xnlo_tm, bffn_dram):
    S = cx.S
    sp, wp = cx.srcpool, cx.wpool
    sp.reset(); wp.reset()
    xnh = FmSrc(cx, "f2xh", xn_tm, sp)
    xnl = FmSrc(cx, "f2xl", xnlo_tm, sp)
    lg, lgkey = emit_sel_logits(cx, "f2g", xnh.get, xnl.get, W["l2_gate_w"], 8)
    maskD = cx.dram_scratch("f2_maskD", (8, S_LEN), BF16)
    scoreD = cx.dram_scratch("f2_scoreD", (1, S_LEN), F32)
    mask_fm = cx.sbuf("f2_mask", (8, 128), BF16)
    score_fm = cx.sbuf("f2_score", (1, 128), F32)
    lgs = cx.sbuf("f2_lgs", (8, 128), F32)
    lt = cx.sbuf("f2_lt", (128, 8), F32)
    z8 = cx.sbuf("f2_z8", (128, 8), F32)
    m1 = cx.sbuf("f2_m1", (128, 1), F32)
    g1 = cx.sbuf("f2_g1", (128, 8), F32)
    f1 = cx.sbuf("f2_f1", (128, 8), F32)
    pt8 = cx.psum("f2_pt8", (128, 8), F32, bank=2)
    pt8b = cx.psum("f2_pt8b", (8, 128), F32, bank=3)
    pt1 = cx.psum("f2_pt1", (1, 128), F32, bank=3)
    idf = cx.sb["ident_f"]
    S.op('dve', lambda E: E.memset(z8[:], 0.0), writes=("f2_z8",))
    for c in range(NTCH):
        S.dma('sp', lgs[:], lg[:, c * 128:(c + 1) * 128], writes=("f2_lgs",))
        S.op('pe', lambda E: E.transpose(pt8[:], lgs[:], idf[0:8, 0:8]),
             reads=("f2_lgs", "c_ident_f"), writes=("PSB2",))
        S.op('dve', lambda E: E.tensor_copy(lt[:], pt8[:]),
             reads=("PSB2",), writes=("f2_lt",))
        S.op('dve', lambda E: E.reduce_max(m1[:], lt[:], axis=AX.X),
             reads=("f2_lt",), writes=("f2_m1",))
        S.op('dve', lambda E: E.tensor_scalar(g1[:], lt[:], m1[:], None, ALU.is_ge),
             reads=("f2_lt", "f2_m1"), writes=("f2_g1",))
        emit_firstone(cx, "f2f", g1[:], "f2_g1", f1, "f2_f1", 8, z8[:], "f2_z8")
        S.op('act', lambda E: E.activation(m1[:], m1[:], AF.Sigmoid),
             reads=("f2_m1",), writes=("f2_m1",))
        S.op('pe', lambda E: E.transpose(pt8b[:], f1[:], idf[:]),
             reads=("f2_f1", "c_ident_f"), writes=("PSB3",))
        S.op('act', lambda E: E.activation(mask_fm[:], pt8b[:], AF.Copy),
             reads=("PSB3",), writes=("f2_mask",))
        S.dma('sp', maskD[:, c * 128:(c + 1) * 128], mask_fm[:], reads=("f2_mask",))
        S.op('pe', lambda E: E.transpose(pt1[:], m1[:], idf[:]),
             reads=("f2_m1", "c_ident_f"), writes=("PSB3",))
        S.op('act', lambda E: E.activation(score_fm[:], pt1[:], AF.Copy),
             reads=("PSB3",), writes=("f2_score",))
        S.dma('sp', scoreD[:, c * 128:(c + 1) * 128], score_fm[:], reads=("f2_score",))
    XS = cx.dram_scratch("f2_XS", (D, S_LEN), BF16)
    psc = cx.psum("f2_psc", (128, 512), F32, bank=2)
    scb = cx.sbuf("f2_scb", (128, 512), F32)
    xsb = cx.sbuf("f2_xsb", (128, 512), BF16)
    scrow = cx.sbuf("f2_scrow", (1, 512), F32)
    mrow = cx.sbuf("f2_mrow", (1, 512), BF16)
    for tt in range(8):
        S.dma('sp', scrow[:], scoreD[:, tt * 512:(tt + 1) * 512], writes=("f2_scrow",))
        S.op('pe', lambda E: E.matmul(psc[:], cx.sb["ones_row_f"][:], scrow[:],
                                      start=True, stop=True),
             reads=("f2_scrow", "c_ones_f"), writes=("PSB2",))
        S.op('act', lambda E: E.activation(scb[:], psc[:], AF.Copy),
             reads=("PSB2",), writes=("f2_scb",))
        for ic in range(NDCH):
            x_ap, xk = xnh.get(ic, tt)
            S.op('dve', lambda E, x_ap=x_ap: E.tensor_mul(xsb[:], x_ap, scb[:]),
                 reads=(xk, "f2_scb"), writes=("f2_xsb",))
            S.dma('sp', XS[ic * 128:(ic + 1) * 128, tt * 512:(tt + 1) * 512],
                  xsb[:], reads=("f2_xsb",))
    ACTall = cx.dram_scratch("f2_ACT", (9 * 2048, S_LEN), BF16)
    psg = cx.psum("f2_psg", (128, 512), F32, bank=2)
    psu = cx.psum("f2_psu", (128, 512), F32, bank=3)
    sg = cx.sbuf("f2_sg", (128, 512), F32)
    au = cx.sbuf("f2_au", (128, 512), BF16)
    mrep = cx.sbuf("f2_mrep", (128, 512), BF16)
    xse = [cx.sbuf(f"f2_xse{i}", (128, 512), BF16) for i in range(NDCH)]
    for (e, half) in [(e, h) for e in range(-1, 8) for h in range(2)]:
        wp.reset()
        gsrcw = W["l2_sh_gate"] if e < 0 else W["l2_rt_gate"][e]
        usrcw = W["l2_sh_up"] if e < 0 else W["l2_rt_up"][e]
        wg_ = WJaxWin(cx, f"f2wg{e}_{half}", gsrcw, 1024, half * 1024, 1024, wp)
        wu_ = WJaxWin(cx, f"f2wu{e}_{half}", usrcw, 1024, half * 1024, 1024, wp)
        for tt in range(8):
            srcs = []
            if e < 0:
                for ic in range(NDCH):
                    srcs.append(xnh.get(ic, tt))
            if e >= 0:
                S.dma('sp', mrow[:], maskD[e:e + 1, tt * 512:(tt + 1) * 512],
                      writes=("f2_mrow",))
                S.op('pe', lambda E: E.matmul(
                        psc[:], cx.sb["ones_row_b"][:], mrow[:],
                        start=True, stop=True),
                     reads=("f2_mrow", "c_ones_b"), writes=("PSB2",))
                S.op('act', lambda E: E.activation(mrep[:], psc[:], AF.Copy),
                     reads=("PSB2",), writes=("f2_mrep",))
                srcs = []
                for ic in range(NDCH):
                    S.dma('sp', xse[ic][:],
                          XS[ic * 128:(ic + 1) * 128, tt * 512:(tt + 1) * 512],
                          writes=(f"f2_xse{ic}",))
                    S.op('dve', lambda E, ic=ic: E.tensor_mul(xse[ic][:], xse[ic][:],
                                                              mrep[:]),
                         reads=(f"f2_xse{ic}", "f2_mrep"), writes=(f"f2_xse{ic}",))
                    srcs.append((xse[ic][:], f"f2_xse{ic}"))
            base = (e + 1) * 2048
            for ch in range(half * 8, half * 8 + 8):
                gts = [wg_.get(ic, ch) for ic in range(NDCH)]
                uts = [wu_.get(ic, ch) for ic in range(NDCH)]

                def fng(E, gts=gts, uts=uts, srcs=srcs):
                    inst = None
                    for i in range(NDCH):
                        E.matmul(psg[:], gts[i][0], srcs[i][0], start=(i == 0),
                                 stop=(i == NDCH - 1))
                        inst = E.matmul(psu[:], uts[i][0], srcs[i][0], start=(i == 0),
                                        stop=(i == NDCH - 1))
                    return inst
                rds = set(k for _, k in srcs)
                rds.update(k for _, k in gts)
                rds.update(k for _, k in uts)
                S.op('pe', fng, reads=tuple(rds), writes=("PSB2", "PSB3"))
                S.op('act', lambda E: E.activation(sg[:], psg[:], AF.Silu),
                     reads=("PSB2",), writes=("f2_sg",))
                S.op('dve', lambda E: E.tensor_mul(au[:], sg[:], psu[:]),
                     reads=("f2_sg", "PSB3"), writes=("f2_au",))
                S.dma('sp', ACTall[base + ch * 128:base + (ch + 1) * 128,
                                   tt * 512:(tt + 1) * 512], au[:], reads=("f2_au",))
    wrow = [cx.sbuf(f"f2_wr{i}", (128, 1024), BF16) for i in range(2)]
    srow = [cx.sbuf(f"f2_sr{i}", (128, 512), BF16) for i in range(2)]
    psall = [cx.psum(f"f2_dn{i}", (128, 512), F32, bank=i) for i in range(8)]
    dob = cx.sbuf("f2_dob", (128, 512), F32)
    dwsrc = [W["l2_sh_down"]] + [W["l2_rt_down"][e] for e in range(8)]
    for tt in range(8):
        for k in range(144):
            ke, ic = divmod(k, 16)
            wb = wrow[k % 2]
            sb = srow[k % 2]
            S.dma('pool', wb[:], dwsrc[ke][ic * 128:(ic + 1) * 128, :],
                  writes=(f"f2_wr{k % 2}",))
            S.dma('sp', sb[:], ACTall[k * 128:(k + 1) * 128,
                                      tt * 512:(tt + 1) * 512],
                  writes=(f"f2_sr{k % 2}",))

            def fnd(E, k=k, wb=wb, sb=sb):
                inst = None
                for oc in range(8):
                    inst = E.matmul(psall[oc][:], wb[:, oc * 128:(oc + 1) * 128],
                                    sb[:], start=(k == 0), stop=(k == 143))
                return inst
            S.op('pe', fnd, reads=(f"f2_wr{k % 2}", f"f2_sr{k % 2}"),
                 writes=tuple(f"PSB{oc}" for oc in range(8)))
        for oc in range(8):
            S.op('act', lambda E, oc=oc: E.activation(dob[:], psall[oc][:], AF.Copy),
                 reads=(f"PSB{oc}",), writes=("f2_dob",))
            S.dma('sp', bffn_dram[oc * 128:(oc + 1) * 128, tt * 512:(tt + 1) * 512],
                  dob[:], reads=("f2_dob",))



L3C = 64.0 ** -0.25
L3EPS = 1e-4


def emit_l3_kproj(cx, W, xn_tm, kD, tagsfx=""):
    sp, wp = cx.srcpool, cx.wpool
    sp.reset(); wp.reset()
    xnsrc = FmSrc(cx, "l3xk" + tagsfx, xn_tm, sp)
    wk = WJax(cx, "l3wk" + tagsfx, W["l3_wk"], 1024, 1024, wp)
    emit_dense_fm(cx, "l3k" + tagsfx, wk.get, xnsrc.get, NDCH, NDCH,
                  sink_to_dram(cx, "l3k" + tagsfx, kD))


def emit_l3_projT(cx, W):
    S = cx.S
    pb = cx.sbuf("l3_pb", (128, 64), BF16)
    pt = cx.psum("l3_ptp", (64, 128), F32, bank=7)
    projT = cx.sbuf("l3_projT", (64, 256), BF16)
    for mh in range(2):
        S.dma('pool', pb[:], W["l3_proj"][mh * 128:(mh + 1) * 128, :],
              writes=("l3_pb",))
        S.op('pe', lambda E: E.matmul(pt[:], pb[:], cx.sb["ident_b"][:],
                                      start=True, stop=True),
             reads=("l3_pb", "c_ident_b"), writes=("l3_ptp",))
        S.op('dve', lambda E, mh=mh: E.tensor_copy(projT[:, mh * 128:(mh + 1) * 128],
                                                   pt[:]),
             reads=("l3_ptp",), writes=("l3_projT",))
    return projT


def emit_l3_negdiag(cx, src, skeys, cols_t, colskey, withstab, negstab, tag):
    S = cx.S
    sqb = cx.sbuf("l3_sqb", (64, 512), F32)
    ndrow = cx.sbuf("l3_ndrow", (1, 512), F32)
    psdg = cx.psum("l3_psdg", (1, 512), F32, bank=1)
    psc1 = cx.psum("l3_psc1", (128, 1), F32, bank=6)
    onesc = cx.sbuf("ones_col_f", (128, 1), F32)
    ones11 = cx.constcol(1.0, 1)
    for j8 in range(8):
        sl = slice(j8 * 512, (j8 + 1) * 512)
        S.op('act', lambda E, sl=sl: E.activation(sqb[:], src[:, sl], AF.Square,
                                                  scale=L3C),
             reads=skeys, writes=("l3_sqb",))
        S.op('pe', lambda E: E.matmul(psdg[:], onesc[0:64, :], sqb[:],
                                      start=True, stop=True),
             reads=("l3_sqb", "c_ones_col"), writes=("PSB1",))
        S.op('act', lambda E: E.activation(ndrow[:], psdg[:], AF.Copy, scale=-0.5),
             reads=("PSB1",), writes=("l3_ndrow",))
        for q in range(4):
            S.op('pe', lambda E, q=q: E.matmul(
                    psc1[:], ndrow[0:1, q * 128:(q + 1) * 128], ones11[:],
                    start=True, stop=True),
                 reads=("l3_ndrow", f"ccol{(1.0, 1)}"), writes=("PSB6",))
            idx = j8 * 4 + q
            if withstab:
                S.op('act', lambda E, idx=idx: E.activation(
                        cols_t[:, idx:idx + 1], psc1[:], AF.Identity,
                        bias=negstab[:]),
                     reads=("PSB6", "l3_negstab"), writes=(colskey,))
            else:
                S.op('act', lambda E, idx=idx: E.activation(
                        cols_t[:, idx:idx + 1], psc1[:], AF.Copy),
                     reads=("PSB6",), writes=(colskey,))


def emit_L3_attn(cx, W, xn_tm, battn_dram, stab_dram):
    S = cx.S
    sp, wp = cx.srcpool, cx.wpool
    qD = cx.dram_scratch("l3_qD", (D, S_LEN), BF16)
    kD = cx.dram_scratch("l3_kD", (D, S_LEN), BF16)
    vD = cx.dram_scratch("l3_vD", (S_LEN, D), BF16)
    AOD = cx.dram_scratch("l3_AOD", (D, S_LEN), BF16)
    sp.reset(); wp.reset()
    xnsrc = FmSrc(cx, "l3xq", xn_tm, sp)
    wq = WJax(cx, "l3wq", W["l3_wq"], 1024, 1024, wp)
    emit_dense_fm(cx, "l3q", wq.get, xnsrc.get, NDCH, NDCH,
                  sink_to_dram(cx, "l3q", qD))
    emit_l3_kproj(cx, W, xn_tm, kD)
    sp.reset(); wp.reset()
    x128 = FmSrc128(cx, "l3xv", xn_tm, sp)
    wv = WJax(cx, "l3wv", W["l3_wv"], 1024, 1024, wp)
    emit_dense_tm(cx, "l3v", lambda ic, oh: wv.get(ic, oh, ow=512), x128.get,
                  NDCH, 2, vD)
    projT = emit_l3_projT(cx, W)
    srow8 = cx.sbuf("l3_srow8", (1, 8), F32)
    s1 = cx.sbuf("l3_s1", (1, 1), F32)
    S.dma('sp', srow8[:], stab_dram[:, :], writes=("l3_srow8",))
    S.op('dve', lambda E: E.reduce_max(s1[:], srow8[:], axis=AX.X),
         reads=("l3_srow8",), writes=("l3_s1",))
    negstab = bcast_col(cx, s1[:], "l3_s1", "l3_negstab", parts=128, scale=-1.0,
                        name="l3_negstab")
    S.alias["l3_negstab"] = "l3_negstab"
    sp.reset(); wp.reset()
    kh_, kkh = sp.alloc(8)
    kh = kh_[0:64, :]
    qh_, kqh = sp.alloc(8)
    qh = qh_[0:64, :]
    qpf = []
    for mh in range(2):
        b_, k_ = sp.alloc(8)
        qpf.append((b_, k_))
    ndk = cx.sbuf("l3_ndk", (128, 32), F32)
    ndq = cx.sbuf("l3_ndq", (128, 32), F32)
    kpt = cx.sbuf("l3_kpt", (128, 256), BF16)
    qpt = cx.sbuf("l3_qpt", (128, 256), BF16)
    vt3 = cx.sbuf("l3_vt", (128, 64), BF16)
    rmx = cx.sbuf("l3_rmx", (128, 1), F32)
    ndc = cx.sbuf("l3_ndc", (128, 1), F32)
    kcr = cx.sbuf("l3_kcr", (1, 256), F32)
    kccol = cx.sbuf("l3_kccol", (128, 2), BF16)
    ctxs = cx.sbuf("l3_ctxs", (128, 128), BF16)
    drow3 = cx.sbuf("l3_drow", (1, 512), F32)
    osb = cx.sbuf("l3_osb", (64, 512), F32)
    obf = cx.sbuf("l3_obf", (64, 512), BF16)
    psdd = cx.psum("l3_psdd", (128, 256), F32, bank=0)
    pskc = cx.psum("l3_pskc", (1, 256), F32, bank=2)
    psctx = [cx.psum(f"l3_psctx{i}", (128, 64), F32, bank=3 + i) for i in range(2)]
    pstr3 = cx.psum("l3_pstr", (128, 128), F32, bank=5)
    psb3 = cx.psum("l3_psb", (64, 512), F32, bank=6)
    psout3 = cx.psum("l3_psout", (64, 512), F32, bank=7)
    psd3 = cx.psum("l3_psd", (1, 512), F32, bank=1)
    onesc = cx.sbuf("ones_col_f", (128, 1), F32)
    onescb = cx.sbuf("ones_col_b", (128, 1), BF16)
    onesr = cx.sbuf("ones_row_f", (1, 128), F32)
    ones11 = cx.constcol(1.0, 1)
    identb = cx.sb["ident_b"]
    for h in range(16):
        S.dma('sp', kh[:], kD[h * 64:(h + 1) * 64, :], writes=kkh)
        S.dma('sp', qh[:], qD[h * 64:(h + 1) * 64, :], writes=kqh)
        emit_l3_negdiag(cx, kh, kkh, ndk, "l3_ndk", True, negstab, "k")
        emit_l3_negdiag(cx, qh, kqh, ndq, "l3_ndq", False, None, "q")
        for t in range(NTCH):
            t0 = t * 128
            S.op('pe', lambda E, t0=t0: E.matmul(psdd[:], kh[:, t0:t0 + 128],
                                                 projT[:], start=True, stop=True),
                 reads=kkh + ("l3_projT",), writes=("PSB0",))
            S.op('act', lambda E, t=t: E.activation(kpt[:], psdd[:], AF.Exp,
                                                    scale=L3C,
                                                    bias=ndk[:, t:t + 1]),
                 reads=("PSB0", "l3_ndk"), writes=("l3_kpt",))
            S.op('dve', lambda E: E.tensor_single_scalar(kpt[:], kpt[:], L3EPS,
                                                         ALU.add),
                 reads=("l3_kpt",), writes=("l3_kpt",))
            S.dma('sp', vt3[:], vD[t0:t0 + 128, h * 64:(h + 1) * 64],
                  writes=("l3_vt",))
            S.op('pe', lambda E, t=t: E.matmul(pskc[:], onescb[:], kpt[:],
                                               start=(t == 0), stop=(t == NTCH - 1)),
                 reads=("l3_kpt", "c_ones_col_b"), writes=("PSB2",))
            for mh in range(2):
                S.op('pe', lambda E, t=t, mh=mh: E.matmul(
                        psctx[mh][:], kpt[:, mh * 128:(mh + 1) * 128], vt3[:],
                        start=(t == 0), stop=(t == NTCH - 1)),
                     reads=("l3_kpt", "l3_vt"), writes=(f"PSB{3 + mh}",))
        S.op('act', lambda E: E.activation(kcr[:], pskc[:], AF.Copy),
             reads=("PSB2",), writes=("l3_kcr",))
        for mh in range(2):
            S.op('pe', lambda E, mh=mh: E.matmul(
                    cx.ps["l3_psc1"][:], kcr[0:1, mh * 128:(mh + 1) * 128],
                    ones11[:], start=True, stop=True),
                 reads=("l3_kcr", f"ccol{(1.0, 1)}"), writes=("PSB6",))
            S.op('act', lambda E, mh=mh: E.activation(kccol[:, mh:mh + 1],
                                                      cx.ps["l3_psc1"][:], AF.Copy),
                 reads=("PSB6",), writes=("l3_kccol",))
            S.op('act', lambda E, mh=mh: E.activation(ctxs[:, mh * 64:(mh + 1) * 64],
                                                      psctx[mh][:], AF.Copy),
                 reads=(f"PSB{3 + mh}",), writes=("l3_ctxs",))
        for t in range(NTCH):
            t0 = t * 128
            S.op('pe', lambda E, t0=t0: E.matmul(psdd[:], qh[:, t0:t0 + 128],
                                                 projT[:], start=True, stop=True),
                 reads=kqh + ("l3_projT",), writes=("PSB0",))
            S.op('dve', lambda E: E.reduce_max(rmx[:], psdd[:], axis=AX.X),
                 reads=("PSB0",), writes=("l3_rmx",))
            S.op('dve', lambda E, t=t: E.scalar_tensor_tensor(
                    ndc[:], rmx[:], -L3C, ndq[:, t:t + 1], ALU.mult, ALU.add),
                 reads=("l3_rmx", "l3_ndq"), writes=("l3_ndc",))
            S.op('act', lambda E: E.activation(qpt[:], psdd[:], AF.Exp,
                                               scale=L3C, bias=ndc[:]),
                 reads=("PSB0", "l3_ndc"), writes=("l3_qpt",))
            S.op('dve', lambda E: E.tensor_single_scalar(qpt[:], qpt[:], L3EPS,
                                                         ALU.add),
                 reads=("l3_qpt",), writes=("l3_qpt",))
            for mh in range(2):
                S.op('pe', lambda E, mh=mh: E.matmul(
                        pstr3[:], qpt[:, mh * 128:(mh + 1) * 128], identb[:],
                        start=True, stop=True),
                     reads=("l3_qpt", "c_ident_b"), writes=("PSB5",))
                S.op('act', lambda E, mh=mh, t0=t0: E.activation(
                        qpf[mh][0][:, t0:t0 + 128], pstr3[:], AF.Copy),
                     reads=("PSB5",), writes=qpf[mh][1])
        for j8 in range(8):
            sl = slice(j8 * 512, (j8 + 1) * 512)

            def fn_d(E, sl=sl):
                inst = None
                for mh in range(2):
                    inst = E.matmul(psd3[:], kccol[:, mh:mh + 1],
                                    qpf[mh][0][:, sl], start=(mh == 0),
                                    stop=(mh == 1))
                return inst
            S.op('pe', fn_d, reads=("l3_kccol",) + qpf[0][1] + qpf[1][1],
                 writes=("PSB1",))
            S.op('act', lambda E: E.activation(drow3[:], psd3[:], AF.Ln),
                 reads=("PSB1",), writes=("l3_drow",))
            S.op('act', lambda E: E.activation(drow3[:], drow3[:], AF.Exp,
                                               scale=-1.0),
                 reads=("l3_drow",), writes=("l3_drow",))
            S.op('pe', lambda E: E.matmul(psb3[:], onesr[:, 0:64], drow3[:],
                                          start=True, stop=True),
                 reads=("l3_drow", "c_ones_f"), writes=("PSB6",))

            def fn_o(E, sl=sl):
                inst = None
                for mh in range(2):
                    inst = E.matmul(psout3[:], ctxs[:, mh * 64:(mh + 1) * 64],
                                    qpf[mh][0][:, sl], start=(mh == 0),
                                    stop=(mh == 1))
                return inst
            S.op('pe', fn_o, reads=("l3_ctxs",) + qpf[0][1] + qpf[1][1],
                 writes=("PSB7",))
            S.op('act', lambda E: E.activation(osb[:], psout3[:], AF.Copy),
                 reads=("PSB7",), writes=("l3_osb",))
            S.op('dve', lambda E: E.tensor_mul(obf[:], osb[:], psb3[:]),
                 reads=("l3_osb", "PSB6"), writes=("l3_obf",))
            S.dma('sp', AOD[h * 64:(h + 1) * 64, sl], obf[:], reads=("l3_obf",))
    sp.reset(); wp.reset()
    asrc = FmDirect(cx, "l3ao", AOD, sp)
    wo = WJax(cx, "l3wo", W["l3_wo"], 1024, 1024, wp)
    bo_cols = load_bias_cols(cx, "l3bo", W["l3_bo"], 1024)
    emit_dense_fm(cx, "l3o", wo.get, asrc.get, NDCH, NDCH,
                  sink_to_dram(cx, "l3o", battn_dram, dtype=F32,
                               bias=(bo_cols, "l3bo_bias")))



def emit_L3_ffn(cx, W, xn_tm, xnlo_tm, bffn_tm):
    S = cx.S
    sp, wp = cx.srcpool, cx.wpool
    sp.reset(); wp.reset()
    CAP, DFFP = 512, 2816
    NFF = DFFP // 128
    wqkvB = cx.dram_scratch("l3_wqkvB", (3072, 1024), BF16)
    woutB = cx.dram_scratch("l3_woutB", (1024, 1024), BF16)
    w1B = cx.dram_scratch("l3_w1B", (DFFP, 1024), BF16)
    w2B = cx.dram_scratch("l3_w2B", (DFFP, 1024), BF16)
    w3B = cx.dram_scratch("l3_w3B", (1024, DFFP), BF16)
    emit_cast_bf16(cx, W["l3_W_qkv"], wqkvB, 3072, 1024, "wqkv")
    emit_cast_bf16(cx, W["l3_W_out"], woutB, 1024, 1024, "wout")
    emit_cast_bf16(cx, W["l3_W1"], w1B, 2730, 1024, "w1")
    emit_cast_bf16(cx, W["l3_W2"], w2B, 2730, 1024, "w2")
    emit_cast_bf16(cx, W["l3_W3"], w3B, 1024, 2730, "w3")
    zpad = cx.sbuf("castbuf", (128, 1024), BF16)
    S.op('dve', lambda E: E.memset(zpad[:], 0.0), writes=("castbuf",))
    S.dma('sp', w1B[2730:DFFP, :], zpad[0:DFFP - 2730, :], reads=("castbuf",))
    S.dma('sp', w2B[2730:DFFP, :], zpad[0:DFFP - 2730, :], reads=("castbuf",))
    for r in range(8):
        S.dma('sp', w3B[r * 128:(r + 1) * 128, 2730:DFFP],
              zpad[:, 0:DFFP - 2730], reads=("castbuf",))
    xnh = FmSrc(cx, "f3x", xn_tm, sp)
    xnl = FmSrc(cx, "f3l", xnlo_tm, sp)
    lg, lgkey = emit_sel_logits(cx, "f3g", xnh.get, xnl.get,
                                W["l3_w_router"].unsqueeze(1), 1)
    sp.reset()
    scb_, kscb = sp.allocf32(16)
    scb = scb_[:, 0:S_LEN]
    gt_, kgt = sp.allocf32(16)
    gtile = gt_[:, 0:S_LEN]
    tokmapD = cx.dram_scratch("l3_tokmap", (CAP, 1), I32)
    srow = cx.sbuf("f3_srow", (1, 512), F32)
    scol = cx.sbuf("f3_scol", (128, 1), F32)
    gcol = cx.sbuf("f3_gcol", (128, 1), F32)
    tcol = cx.sbuf("f3_tcol", (128, 1), F32)
    rkI = cx.sbuf("f3_rkI", (128, 1), I32)
    tkI = cx.sbuf("f3_tkI", (128, 1), I32)
    psbc = cx.psum("f3_psbc", (128, 512), F32, bank=0)
    psc1f = cx.psum("f3_psc1", (128, 1), F32, bank=1)
    ones11 = cx.constcol(1.0, 1)
    onesr = cx.sbuf("ones_row_f", (1, 128), F32)
    trilo = cx.sb["trilo"]
    iotac = cx.sb["iota128"]
    for j8 in range(8):
        S.dma('sp', srow[:], lg[:, j8 * 512:(j8 + 1) * 512], writes=("f3_srow",))
        S.op('pe', lambda E: E.matmul(psbc[:], onesr[:], srow[:],
                                      start=True, stop=True),
             reads=("f3_srow", "c_ones_f"), writes=("PSB0",))
        S.op('act', lambda E, j8=j8: E.activation(scb[:, j8 * 512:(j8 + 1) * 512],
                                                  psbc[:], AF.Copy),
             reads=("PSB0",), writes=kscb)
    for ci in range(NTCH):
        S.dma('sp', srow[0:1, 0:128], lg[:, ci * 128:(ci + 1) * 128],
              writes=("f3_srow",))
        S.op('pe', lambda E: E.matmul(psc1f[:], srow[0:1, 0:128], ones11[:],
                                      start=True, stop=True),
             reads=("f3_srow", f"ccol{(1.0, 1)}"), writes=("PSB1",))
        S.op('act', lambda E: E.activation(scol[:], psc1f[:], AF.Copy),
             reads=("PSB1",), writes=("f3_scol",))
        S.op('dve', lambda E: E.tensor_scalar(gtile[:], scb[:], scol[:], None,
                                              ALU.is_gt),
             reads=kscb + ("f3_scol",), writes=kgt)
        S.op('dve', lambda E: E.reduce_sum(gcol[:], gtile[:], axis=AX.X),
             reads=kgt, writes=("f3_gcol",))
        w = (ci + 1) * 128
        S.op('dve', lambda E, w=w: E.tensor_scalar(gtile[:, 0:w], scb[:, 0:w],
                                                   scol[:], None, ALU.is_equal),
             reads=kscb + ("f3_scol",), writes=kgt)
        S.op('dve', lambda E, ci=ci: E.tensor_mul(
                gtile[:, ci * 128:(ci + 1) * 128],
                gtile[:, ci * 128:(ci + 1) * 128], trilo[:]),
             reads=kgt + ("c_trilo",), writes=kgt)
        S.op('dve', lambda E, w=w: E.reduce_sum(tcol[:], gtile[:, 0:w], axis=AX.X),
             reads=kgt, writes=("f3_tcol",))
        S.op('dve', lambda E: E.tensor_add(gcol[:], gcol[:], tcol[:]),
             reads=("f3_gcol", "f3_tcol"), writes=("f3_gcol",))
        S.op('dve', lambda E: E.tensor_copy(rkI[:], gcol[:]),
             reads=("f3_gcol",), writes=("f3_rkI",))
        S.op('dve', lambda E, ci=ci: E.tensor_single_scalar(
                tcol[:], iotac[:], float(ci * 128), ALU.add),
             reads=("c_iota",), writes=("f3_tcol",))
        S.op('dve', lambda E: E.tensor_copy(tkI[:], tcol[:]),
             reads=("f3_tcol",), writes=("f3_tkI",))
        S.idma(tokmapD, IndirectOffsetOnAxis(ap=rkI[:].bitcast(U32), axis=0),
               tkI[:], None, bounds_check=CAP - 1, oob_is_err=False,
               reads=("f3_rkI", "f3_tkI"))
    sp.reset()
    g1b_, kg1b = sp.allocf32(4)
    g1b = g1b_[:, 0:D]
    g2b_, kg2b = sp.allocf32(4)
    g2b = g2b_[:, 0:D]
    for (gb, kgb, wsrc) in ((g1b, kg1b, W["l3_g1"]), (g2b, kg2b, W["l3_g2"])):
        for j in range(2):
            S.dma('sp', srow[:], wsrc[j * 512:(j + 1) * 512].unsqueeze(0),
                  writes=("f3_srow",))
            S.op('pe', lambda E: E.matmul(psbc[:], onesr[:], srow[:],
                                          start=True, stop=True),
                 reads=("f3_srow", "c_ones_f"), writes=("PSB0",))
            S.op('act', lambda E, gb=gb, j=j: E.activation(
                    gb[:, j * 512:(j + 1) * 512], psbc[:], AF.Copy),
                 reads=("PSB0",), writes=kgb)
    xs, xsk = [], []
    for rt in range(4):
        b_, k_ = sp.allocf32(4)
        xs.append(b_[:, 0:D]); xsk.append(k_)
    hfm_, khfm = sp.alloc(8)
    hfm = [hfm_[:, dc * 512:(dc + 1) * 512] for dc in range(NDCH)]
    tkb = cx.sbuf("f3_tkb", (128, 1), I32)
    xgh = cx.sbuf("f3_xgh", (128, 1024), BF16)
    xgl = cx.sbuf("f3_xgl", (128, 1024), BF16)
    tmpf = cx.sbuf("f3_tmpf", (128, 1024), F32)
    hbt = cx.sbuf("f3_hbt", (128, 1024), BF16)
    mcol = cx.sbuf("f3_mcol", (128, 1), F32)
    rcol = cx.sbuf("f3_rcol", (128, 1), F32)
    eps6 = cx.constcol(1e-6)
    pstrf = cx.psum("f3_pstr", (128, 128), F32, bank=5)
    identb = cx.sb["ident_b"]

    def rms_tiles(src_t, srckey, gb, kgb, outfm, koutfm, rt):
        S.op('act', lambda E: E.activation(tmpf[:], src_t[:], AF.Square),
             reads=srckey, writes=("f3_tmpf",))
        S.op('dve', lambda E: E.reduce_sum(mcol[:], tmpf[:], axis=AX.X),
             reads=("f3_tmpf",), writes=("f3_mcol",))
        S.op('act', lambda E: E.activation(rcol[:], mcol[:], AF.Ln,
                                           scale=1.0 / D, bias=eps6[:]),
             reads=("f3_mcol", f"ccol{(1e-06, 128)}"), writes=("f3_rcol",))
        S.op('act', lambda E: E.activation(rcol[:], rcol[:], AF.Exp, scale=-0.5),
             reads=("f3_rcol",), writes=("f3_rcol",))
        S.op('dve', lambda E: E.tensor_scalar(tmpf[:], src_t[:], rcol[:], None,
                                              ALU.mult),
             reads=srckey + ("f3_rcol",), writes=("f3_tmpf",))
        S.op('dve', lambda E: E.tensor_mul(hbt[:], tmpf[:], gb[:]),
             reads=("f3_tmpf",) + kgb, writes=("f3_hbt",))
        for dc in range(NDCH):
            S.op('pe', lambda E, dc=dc: E.matmul(
                    pstrf[:], hbt[:, dc * 128:(dc + 1) * 128], identb[:],
                    start=True, stop=True),
                 reads=("f3_hbt", "c_ident_b"), writes=("PSB5",))
            S.op('act', lambda E, dc=dc, rt=rt: E.activation(
                    outfm[dc][:, rt * 128:(rt + 1) * 128], pstrf[:], AF.Copy),
                 reads=("PSB5",), writes=koutfm)

    for rt in range(4):
        S.dma('sp', tkb[:], tokmapD[rt * 128:(rt + 1) * 128, :], writes=("f3_tkb",))
        S.idma(xgh[:], None, xn_tm,
               IndirectOffsetOnAxis(ap=tkb[:].bitcast(U32), axis=0),
               bounds_check=S_LEN - 1, oob_is_err=False,
               reads=("f3_tkb",), writes=("f3_xgh",))
        S.idma(xgl[:], None, xnlo_tm,
               IndirectOffsetOnAxis(ap=tkb[:].bitcast(U32), axis=0),
               bounds_check=S_LEN - 1, oob_is_err=False,
               reads=("f3_tkb",), writes=("f3_xgl",))
        S.op('dve', lambda E, rt=rt: E.tensor_add(xs[rt][:], xgh[:], xgl[:]),
             reads=("f3_xgh", "f3_xgl"), writes=xsk[rt])
        rms_tiles(xs[rt], tuple(xsk[rt]), g1b, tuple(kg1b), hfm, khfm, rt)
    qf_, kqf = sp.alloc(8)
    qf = [qf_[:, dc * 512:(dc + 1) * 512] for dc in range(NDCH)]
    kf_, kkf = sp.alloc(8)
    kf = [kf_[:, dc * 512:(dc + 1) * 512] for dc in range(NDCH)]
    vtm, kvtm = [], []
    for rt in range(4):
        b_, k_ = sp.alloc(2)
        vtm.append(b_); kvtm.append(k_)
    psq = [cx.psum(f"f3_psq{i}", (128, 512), F32, bank=i) for i in range(2)]
    sb512 = cx.sbuf("f3_sb512", (128, 512), F32)
    for (part, dst, kdst) in ((0, qf, kqf), (1, kf, kkf)):
        wp.reset()
        wt = WTorch(cx, f"f3wqkv{part}", wqkvB, 1024, 3072, wp)
        for nc8 in range(NDCH):
            oc = part * 8 + nc8
            mms = [(wt.get(ic, oc), ic) for ic in range(NDCH)]

            def fn(E, mms=mms, nc8=nc8):
                p = psq[nc8 % 2]
                inst = None
                for i, ((w_ap, wk), ic) in enumerate(mms):
                    inst = E.matmul(p[:], w_ap, hfm[ic], start=(i == 0),
                                    stop=(i == NDCH - 1))
                return inst
            rds = tuple(set([m[0][1] for m in mms])) + khfm
            S.op('pe', fn, reads=rds, writes=(f"PSB{nc8 % 2}",))
            S.op('act', lambda E, nc8=nc8, dst=dst: E.activation(
                    dst[nc8][:], psq[nc8 % 2][:], AF.Copy),
                 reads=(f"PSB{nc8 % 2}",), writes=kdst)
    wp.reset()
    wtv = WTorch(cx, "f3wqkv2", wqkvB, 1024, 3072, wp)
    psv = cx.psum("f3_psv", (128, 128), F32, bank=2)
    for rt in range(4):
        for vc8 in range(NDCH):
            mms = [(wtv.get(ic, 16 + vc8), ic) for ic in range(NDCH)]

            def fnv(E, mms=mms, rt=rt):
                inst = None
                for i, ((w_ap, wk), ic) in enumerate(mms):
                    inst = E.matmul(psv[:], hfm[ic][:, rt * 128:(rt + 1) * 128],
                                    w_ap, start=(i == 0), stop=(i == NDCH - 1))
                return inst
            rds = tuple(set([m[0][1] for m in mms])) + khfm
            S.op('pe', fnv, reads=rds, writes=("PSB2",))
            S.op('act', lambda E, rt=rt, vc8=vc8: E.activation(
                    vtm[rt][:, vc8 * 128:(vc8 + 1) * 128], psv[:], AF.Copy),
                 reads=("PSB2",), writes=kvtm[rt])
    aof_, kaof = sp.alloc(8)
    aof = [aof_[:, dc * 512:(dc + 1) * 512] for dc in range(NDCH)]
    expb_, kexpb = sp.alloc(4)
    expb = expb_[:, 0:2048]
    pssc = cx.psum("f3_pssc", (128, 512), F32, bank=0)
    psl3 = cx.psum("f3_psl", (1, 512), F32, bank=1)
    psao = cx.psum("f3_psao", (64, 512), F32, bank=2)
    psb2 = cx.psum("f3_psb2", (64, 512), F32, bank=3)
    lr3 = cx.sbuf("f3_lr", (1, 512), F32)
    osb3 = cx.sbuf("f3_osb", (64, 512), F32)
    onescb = cx.sbuf("ones_col_b", (128, 1), BF16)
    triuneg = cx.sb["triuneg"]
    for h in range(16):
        hc, hr = h // 2, (h % 2) * 64
        for kt in range(4):
            def fsc(E, kt=kt, hc=hc, hr=hr):
                E.matmul(pssc[:], kf[hc][hr:hr + 64, kt * 128:(kt + 1) * 128],
                         qf[hc][hr:hr + 64, :], start=True, stop=False)
                return E.matmul(pssc[:, kt * 128:(kt + 1) * 128],
                                identb[:], triuneg[:], start=False, stop=True)
            S.op('pe', fsc, reads=kkf + kqf + ("c_ident_b", "c_triuneg"),
                 writes=("PSB0",))
            S.op('act', lambda E, kt=kt: E.activation(
                    expb[:, kt * 512:(kt + 1) * 512], pssc[:], AF.Exp,
                    scale=0.125),
                 reads=("PSB0",), writes=kexpb)
            if kt > 0:
                S.op('dve', lambda E, kt=kt: E.memset(
                        expb[:, kt * 512:kt * 512 + kt * 128], 0.0),
                     writes=kexpb)
        def fl(E):
            inst = None
            for kt in range(4):
                inst = E.matmul(psl3[:], onescb[:],
                                expb[:, kt * 512:(kt + 1) * 512],
                                start=(kt == 0), stop=(kt == 3))
            return inst
        S.op('pe', fl, reads=kexpb + ("c_ones_col_b",), writes=("PSB1",))
        S.op('act', lambda E: E.activation(lr3[:], psl3[:], AF.Ln),
             reads=("PSB1",), writes=("f3_lr",))
        S.op('act', lambda E: E.activation(lr3[:], lr3[:], AF.Exp, scale=-1.0),
             reads=("f3_lr",), writes=("f3_lr",))

        def fav(E, h=h):
            inst = None
            for kt in range(4):
                inst = E.matmul(psao[:], vtm[kt][:, h * 64:(h + 1) * 64],
                                expb[:, kt * 512:(kt + 1) * 512],
                                start=(kt == 0), stop=(kt == 3))
            return inst
        S.op('pe', fav, reads=kexpb + tuple(kvtm[0] + kvtm[1] + kvtm[2] + kvtm[3]),
             writes=("PSB2",))
        S.op('pe', lambda E: E.matmul(psb2[:], onesr[:, 0:64], lr3[:],
                                      start=True, stop=True),
             reads=("f3_lr", "c_ones_f"), writes=("PSB3",))
        S.op('act', lambda E: E.activation(osb3[:], psao[:], AF.Copy),
             reads=("PSB2",), writes=("f3_osb",))
        S.op('dve', lambda E, hc=hc, hr=hr: E.tensor_mul(
                aof[hc][hr:hr + 64, :], osb3[:], psb2[:]),
             reads=("f3_osb", "PSB3"), writes=kaof)
    wp.reset()
    wto = WTorch(cx, "f3wo", woutB, 1024, 1024, wp)
    psxa = [cx.psum(f"f3_psxa{i}", (128, 128), F32, bank=4 + i) for i in range(2)]
    for rt in range(4):
        for oc8 in range(NDCH):
            mms = [(wto.get(din, oc8), din) for din in range(NDCH)]

            def fxa(E, mms=mms, rt=rt, oc8=oc8):
                p = psxa[oc8 % 2]
                inst = None
                for i, ((w_ap, wk), din) in enumerate(mms):
                    inst = E.matmul(p[:], aof[din][:, rt * 128:(rt + 1) * 128],
                                    w_ap, start=(i == 0), stop=(i == NDCH - 1))
                return inst
            rds = tuple(set([m[0][1] for m in mms])) + kaof
            S.op('pe', fxa, reads=rds, writes=(f"PSB{4 + oc8 % 2}",))
            S.op('dve', lambda E, rt=rt, oc8=oc8: E.tensor_add(
                    xs[rt][:, oc8 * 128:(oc8 + 1) * 128],
                    xs[rt][:, oc8 * 128:(oc8 + 1) * 128], psxa[oc8 % 2][:]),
                 reads=(f"PSB{4 + oc8 % 2}",) + tuple(xsk[rt]), writes=xsk[rt])
    h2fm_, kh2fm = sp.alloc(8)
    h2fm = [h2fm_[:, dc * 512:(dc + 1) * 512] for dc in range(NDCH)]
    for rt in range(4):
        rms_tiles(xs[rt], tuple(xsk[rt]), g2b, tuple(kg2b), h2fm, kh2fm, rt)
    a2_, ka2 = sp.alloc(NFF)
    a2 = [a2_[:, f * 512:(f + 1) * 512] for f in range(NFF)]
    psg3 = cx.psum("f3_psg", (128, 512), F32, bank=0)
    psu3 = cx.psum("f3_psu", (128, 512), F32, bank=1)
    sg3 = cx.sbuf("f3_sg", (128, 512), F32)
    for ffc in range(NFF):
        wp.reset()
        w1t = WTorch(cx, f"f3w1_{ffc}", w1B, 1024, DFFP, wp)
        w2t = WTorch(cx, f"f3w2_{ffc}", w2B, 1024, DFFP, wp)
        mg = [(w1t.get(ic, ffc), ic) for ic in range(NDCH)]
        mu = [(w2t.get(ic, ffc), ic) for ic in range(NDCH)]

        def fgu(E, mg=mg, mu=mu):
            inst = None
            for i, ((w_ap, wk), ic) in enumerate(mg):
                E.matmul(psg3[:], w_ap, h2fm[ic], start=(i == 0),
                         stop=(i == NDCH - 1))
            for i, ((w_ap, wk), ic) in enumerate(mu):
                inst = E.matmul(psu3[:], w_ap, h2fm[ic], start=(i == 0),
                                stop=(i == NDCH - 1))
            return inst
        rds = tuple(set([m[0][1] for m in mg] + [m[0][1] for m in mu])) + kh2fm
        S.op('pe', fgu, reads=rds, writes=("PSB0", "PSB1"))
        S.op('act', lambda E: E.activation(sg3[:], psg3[:], AF.Silu),
             reads=("PSB0",), writes=("f3_sg",))
        S.op('dve', lambda E, ffc=ffc: E.tensor_mul(a2[ffc][:], sg3[:], psu3[:]),
             reads=("f3_sg", "PSB1"), writes=(ka2[ffc],))
    psx = [cx.psum(f"f3_psx{i}", (128, 512), F32, bank=i) for i in range(8)]
    for ffc in range(NFF):
        wp.reset()
        w3t = WTorch(cx, f"f3w3_{ffc}", w3B, DFFP, 1024, wp)
        for rt in range(4):
            for oc8 in range(NDCH):
                bnk = rt * 2 + oc8 // 4
                qd = oc8 % 4
                w_ap, wk = w3t.get(ffc, oc8)
                S.op('pe', lambda E, bnk=bnk, qd=qd, w_ap=w_ap, ffc=ffc, rt=rt:
                     E.matmul(psx[bnk][:, qd * 128:(qd + 1) * 128],
                              a2[ffc][:, rt * 128:(rt + 1) * 128], w_ap,
                              start=(ffc == 0 and qd == 0),
                              stop=(ffc == NFF - 1 and qd == 3)),
                     reads=(wk, ka2[ffc]), writes=(f"PSB{bnk}",))
    for rt in range(4):
        for half in range(2):
            bnk = rt * 2 + half
            S.op('dve', lambda E, rt=rt, half=half, bnk=bnk: E.tensor_add(
                    xs[rt][:, half * 512:(half + 1) * 512],
                    xs[rt][:, half * 512:(half + 1) * 512], psx[bnk][:]),
                 reads=(f"PSB{bnk}",) + tuple(xsk[rt]), writes=xsk[rt])
    bfh = cx.sbuf("f3_bfh", (128, 1024), BF16)
    bfl = cx.sbuf("f3_bfl", (128, 1024), BF16)
    bff = cx.sbuf("f3_bff", (128, 1024), F32)
    for c in range(NTCH):
        S.dma('sp', bfh[:], xn_tm[c * 128:(c + 1) * 128, :], writes=("f3_bfh",))
        S.dma('sp', bfl[:], xnlo_tm[c * 128:(c + 1) * 128, :], writes=("f3_bfl",))
        S.op('dve', lambda E: E.tensor_add(bff[:], bfh[:], bfl[:]),
             reads=("f3_bfh", "f3_bfl"), writes=("f3_bff",))
        S.dma('sp', bffn_tm[c * 128:(c + 1) * 128, :], bff[:], reads=("f3_bff",))
    for rt in range(4):
        S.dma('sp', tkb[:], tokmapD[rt * 128:(rt + 1) * 128, :], writes=("f3_tkb",))
        S.idma(bffn_tm, IndirectOffsetOnAxis(ap=tkb[:].bitcast(U32), axis=0),
               xs[rt][:], None, bounds_check=S_LEN - 1, oob_is_err=False,
               reads=("f3_tkb",) + tuple(xsk[rt]))
